# version 06
# baseline (speedup 1.0000x reference)
import os
import numpy as np
import ml_dtypes
from contextlib import ExitStack
KSUB = int(os.environ.get("KSUB", "9"))
KSCAN = int(os.environ.get("KSCAN", "512"))
KZ = int(os.environ.get("KZ", "1"))

import concourse.bass as bass
import concourse.mybir as mybir
from concourse.bass_utils import run_bass_kernel_spmd

F32 = mybir.dt.float32
BF16 = mybir.dt.bfloat16
U8 = mybir.dt.uint8
I32 = mybir.dt.int32
AF = mybir.ActivationFunctionType
ALU = mybir.AluOpType
AX = mybir.AxisListType.X
BF = ml_dtypes.bfloat16

B, S, D = 2, 8192, 2048
H, DH, MBL = 16, 128, 16
E, I_MOE, I_SH = 16, 1408, 5632
TOK = 2048
NT = TOK // 128
KC = D // 128
NTG = TOK // 512
MC = I_MOE // 128
SHC = I_SH // 128
NCHUNK = S // MBL
CPR = TOK // MBL
INV_N = 2.0 / (MBL * DH)
HL = 4
NSUB = NCHUNK // 16

C_INVD, C_EPS, C_NEG1, C_TINY, C_NBIG, C_ONE, C_NINV, C_INVN = range(8)

IXC_KFM = 0
IXC_KTM = 16
IXC_Z = 32
N_IDX = 32 + 256


class Ctx:
    def __init__(self, nc, es):
        self.nc, self.es = nc, es
        self.sem, self.cnt = {}, {}

    def sem_new(self, *names):
        for n in names:
            self.sem[n] = self.es.enter_context(self.nc.semaphore(n))
            self.cnt[n] = 0

    def inc(self, inst, name, n=1):
        inst.then_inc(self.sem[name], n)
        self.cnt[name] += n
        return self.cnt[name]

    def core_switch(self, eng, body_fn):
        snap = dict(self.cnt)
        final = None
        for arm in self.nc.switch_core_id(engines=[eng], n=128):
            for k in self.cnt:
                self.cnt[k] = snap[k]
            body_fn(arm.logical % 8)
            final = dict(self.cnt)
        self.cnt.update(final)

    def dma(self, eng, sem_name, out, in_, pre_wait=True, **kw):
        if pre_wait:
            eng.wait_ge(self.sem[sem_name], self.cnt[sem_name])
        return self.inc(eng.dma_start(out=out, in_=in_, **kw), sem_name, 16)


class Arena:
    def __init__(self, t, nbytes):
        self.t, self.nbytes, self.off = t, nbytes, 0

    def reset(self):
        self.off = 0
        return self

    def mark(self):
        return self.off

    def rollback(self, m):
        self.off = m

    def take(self, cols, dtype):
        sz = cols * mybir.dt.size(dtype)
        sz_al = (sz + 31) & ~31
        assert self.off + sz_al <= self.nbytes, f"arena overflow {self.off}+{sz_al}"
        v = self.t[:, self.off:self.off + sz].bitcast(dtype)
        self.off += sz_al
        return v


def build_nc(max_phase=6, debug=False):
    nc = bass.Bass("TRN2", target_bir_lowering=False)
    es = ExitStack()
    c = Ctx(nc, es)
    sm, ct = c.sem, c.cnt

    dT = lambda n, sh, dt=F32, kind="ExternalInput": nc.dram_tensor(n, sh, dt, kind=kind)
    x_in = dT("x_in", [TOK, D])
    wk_t = dT("wk_t", [D, D])
    wv_t = dT("wv_t", [D, D])
    wo_t = dT("wo_t", [D, D])
    rw_t = dT("rw_t", [D, E])
    wg_in = dT("wg_in", [E, D, I_MOE])
    wu_in = dT("wu_in", [E, D, I_MOE])
    wd_in = dT("wd_in", [E, I_MOE, D])
    shg_in = dT("shg_in", [D, I_SH])
    shu_in = dT("shu_in", [D, I_SH])
    shd_in = dT("shd_in", [I_SH, D])
    sgw_in = dT("sgw_in", [128, KC])
    gam_in = dT("gam_in", [128, D])
    bet_in = dT("bet_in", [128, D])
    cos_in = dT("cos_in", [128, TOK])
    sin_in = dT("sin_in", [128, TOK])
    ident_in = dT("ident_in", [128, 128])
    ropeP_in = dT("ropeP_in", [128, 128])
    ones_in = dT("ones_in", [1, 128])
    consts_in = dT("consts_in", [128, 8])
    out_t = dT("out", [TOK, D], F32, kind="ExternalOutput")

    kfm_l = nc.dram_tensor("kfm_l", [H, 128, TOK], BF16)
    ktm_l = nc.dram_tensor("ktm_l", [H, 16, CPR, 128], BF16)
    vtm_l = nc.dram_tensor("vtm_l", [H, 16, CPR, 128], BF16)
    kfm_g = nc.dram_tensor("kfm_g", [8, H, 128, TOK], BF16)
    ktm_g = nc.dram_tensor("ktm_g", [8, H, 16, CPR, 128], BF16)
    vtm_g = nc.dram_tensor("vtm_g", [8, H, 16, CPR, 128], BF16)
    z_l = nc.dram_tensor("z_l", [HL, 16, NCHUNK, 128], BF16)
    z_gt = nc.dram_tensor("z_gt", [8, HL, 16, NCHUNK, 128], BF16)
    routed_d = nc.dram_tensor("routed_d", [D, TOK], F32)

    kfm_rows = kfm_g.ap().rearrange("r h d t -> (r h d) t")
    ktm_rows = ktm_g.ap().rearrange("r h p c e -> (r h p) (c e)")
    vtm_rows = vtm_g.ap().rearrange("r h p c e -> (r h p) (c e)")
    z_rows = z_gt.ap().rearrange("r l p c e -> (r l p c) e")

    dbg = {}
    if debug:
        dbg["xn"] = dT("dbg_xn", [128, KC * TOK], F32, kind="ExternalOutput")
        dbg["comb"] = dT("dbg_comb", [16, TOK], F32, kind="ExternalOutput")
        if max_phase >= 1:
            dbg["kfm"] = dT("dbg_kfm", [H, 128, TOK], BF16, kind="ExternalOutput")
            dbg["ktm"] = dT("dbg_ktm", [H, 16, CPR, 128], BF16, kind="ExternalOutput")
            dbg["vtm"] = dT("dbg_vtm", [H, 16, CPR, 128], BF16, kind="ExternalOutput")
        if max_phase >= 3:
            dbg["zl"] = dT("dbg_zl", [HL, 16, NCHUNK, 128], BF16, kind="ExternalOutput")
        if max_phase >= 5:
            dbg["routed"] = dT("dbg_routed", [D, TOK], F32, kind="ExternalOutput")

    ARENA_B = 132 * 1024
    arena_t = es.enter_context(nc.sbuf_tensor("arena", [128, ARENA_B], U8))
    ar = Arena(arena_t, ARENA_B)
    Sb = lambda n, sh, dt: es.enter_context(nc.sbuf_tensor(n, sh, dt))
    xn = Sb("xn", [128, KC * TOK], BF16)
    consts = Sb("consts", [128, 8], F32)
    ident_f = Sb("ident_f", [128, 128], F32)
    ident_b = Sb("ident_b", [128, 128], BF16)
    ropeP = Sb("ropeP", [128, 128], BF16)
    ones_r = Sb("ones_r", [1, 128], F32)
    stat = Sb("stat", [128, 8], F32)
    comb_tm = Sb("comb_tm", [128, NT * E], F32)
    comb_fm = Sb("comb_fm", [16, TOK], F32)
    rw_sb = Sb("rw_sb", [128, KC * E], BF16)
    sgw_sb = Sb("sgw_sb", [128, KC], BF16)

    Pb = lambda n, sh, dt: es.enter_context(nc.psum_tensor(n, sh, dt))
    p_a = Pb("p_a", [128, 512], F32)
    p_b = Pb("p_b", [128, 512], F32)
    p_c = Pb("p_c", [128, 512], F32)
    p_e = Pb("p_e", [128, 512], F32)
    p_f = Pb("p_f", [128, 512], F32)
    p_w0 = Pb("p_w0", [128, 128], F32)
    p_w1 = Pb("p_w1", [128, 128], F32)
    p_z = Pb("p_z", [128, 128], F32)
    p_wb = [p_w0, p_w1]
    pdn = [p_c, p_e]
    trb = [p_z, p_w0]

    c.sem_new("d_ms", "d_w", "d_cr",
              "d_zl", "d_xr", "d_out", "cc",
              "s_sq", "v_st", "s_sd", "v_g", "v_nm", "s_xn", "t_tr", "v_xn",
              "t_rt", "v_cb", "t_ct", "v_cf", "s_lq",
              "t_k", "v_k1", "t_pm", "v_k2", "t_tk", "v_k3", "t_v", "v_v",
              "t_z0", "t_z1", "v_dz0", "v_dz1", "s_zc0", "s_zc1",
              "t_g0", "t_g1", "s_w0", "s_w1",
              "t_bc", "v_cs", "t_mg", "t_mu", "s_si", "v_su", "v_te",
              "t_dn", "s_ev", "v_sh",
              "t_sg", "s_sg", "v_sg",
              "t_wo", "v_as", "t_ft", "v_ot", "v_ox", "d_dbg", "v_w0", "v_w1",
              "d_wk0", "d_wk1", "d_wv0", "d_wv1", "d_gu0", "d_gu1",
              "d_dn0", "d_dn1", "d_wo0", "d_wo1", "d_ac0", "d_ac1",
              "kf00", "kf01", "kf10", "kf11",
              "kc00", "kc01", "kc10", "kc11",
              "vc00", "vc01", "vc10", "vc11",
              "d_x0", "d_x1", "d_cs", "d_gb",
              "d_kr0", "d_kr1", "d_kt0", "d_kt1", "d_vt0", "d_vt1",
              "zw00", "zw01", "zw10", "zw11", "d_rs0", "d_rs1")

    with es, nc.Block() as block:
        x_tile = [ar.take(D, F32) for _ in range(2)]
        xsq = ar.take(D, F32)
        xn_tm = [ar.take(D, F32) for _ in range(2)]

        @block.sync
        def _(sy):
            c.dma(sy, "d_ms", consts[:], consts_in.ap())
            c.dma(sy, "d_ms", ident_f[:], ident_in.ap())
            c.dma(sy, "d_ms", ones_r[:], ones_in.ap())
            for T in range(NT):
                if T >= 2:
                    sy.wait_ge(sm["s_xn"], T - 1)
                c.dma(sy, f"d_x{T % 2}", x_tile[T % 2], x_in.ap()[T * 128:(T + 1) * 128, :])

        @block.gpsimd
        def _(g):
            c.dma(g, "d_w", ident_b[:], ident_in.ap())
            c.dma(g, "d_w", ropeP[:], ropeP_in.ap())
            c.dma(g, "d_w", rw_sb[:].rearrange("p (k e) -> p k e", k=KC),
                  rw_t.ap().rearrange("(k p) e -> p k e", p=128))
            c.dma(g, "d_w", sgw_sb[:], sgw_in.ap())

        MS_PRE = ct["d_ms"]

        @block.scalar
        def _(s):
            for T in range(NT):
                s.wait_ge(sm[f"d_x{T % 2}"], 16 * (T // 2 + 1))
                c.inc(s.square(xsq, x_tile[T % 2]), "s_sq", 1)
                s.wait_ge(sm["v_st"], T + 1)
                c.inc(s.activation(stat[:, 4:5], stat[:, 6:7], AF.Sqrt,
                                   bias=consts[:, C_EPS:C_EPS + 1]), "s_sd", 1)
                s.wait_ge(sm["v_nm"], T + 1)
                if T >= 2:
                    s.wait_ge(sm["t_tr"], 16 * (T - 1))
                c.inc(s.activation(xn_tm[T % 2], x_tile[T % 2], AF.Identity,
                                   bias=stat[:, 6:7], scale=stat[:, 5:6]), "s_xn", 1)

        @block.vector
        def _(v):
            v.wait_ge(sm["d_ms"], MS_PRE)
            for T in range(NT):
                v.wait_ge(sm[f"d_x{T % 2}"], 16 * (T // 2 + 1))
                def G(inst):
                    gvv = c.inc(inst, "v_g", 1)
                    v.wait_ge(sm["v_g"], gvv)
                G(v.reduce_sum(stat[:, 0:1], x_tile[T % 2], axis=AX))
                v.wait_ge(sm["s_sq"], T + 1)
                G(v.reduce_sum(stat[:, 1:2], xsq, axis=AX))
                G(v.tensor_scalar(stat[:, 2:3], stat[:, 0:1], consts[:, C_INVD:C_INVD + 1], None, ALU.mult))
                G(v.tensor_scalar(stat[:, 3:4], stat[:, 1:2], consts[:, C_INVD:C_INVD + 1], None, ALU.mult))
                G(v.tensor_mul(stat[:, 7:8], stat[:, 2:3], stat[:, 2:3]))
                c.inc(v.tensor_sub(stat[:, 6:7], stat[:, 3:4], stat[:, 7:8]), "v_st", 1)
                v.wait_ge(sm["s_sd"], T + 1)
                G(v.reciprocal(stat[:, 5:6], stat[:, 4:5]))
                c.inc(v.tensor_scalar(stat[:, 6:7], stat[:, 2:3], stat[:, 5:6],
                                      consts[:, C_NEG1:C_NEG1 + 1], ALU.mult, ALU.mult), "v_nm", 1)
                for k in range(KC):
                    if KSUB < 1:
                        break
                    v.wait_ge(sm["t_tr"], 16 * T + k + 1)
                    c.inc(v.tensor_copy(xn[:, k * TOK + T * 128: k * TOK + (T + 1) * 128],
                                        trb[k % 2][:, :]), "v_xn", 1)

        @block.tensor
        def _(t):
            t.wait_ge(sm["d_ms"], MS_PRE)
            t.wait_ge(sm["d_w"], 64)
            for T in range(NT):
                if KSUB < 1:
                    break
                t.wait_ge(sm["s_xn"], T + 1)
                for k in range(KC):
                    idx = 16 * T + k
                    if idx >= 1:
                        t.wait_ge(sm["v_xn"], idx - 1)
                    c.inc(t.transpose(trb[idx % 2][:, :],
                                      xn_tm[T % 2][:, k * 128:(k + 1) * 128], ident_f[:]), "t_tr", 1)

        probs = ar.take(E, F32)
        pn = ar.take(E, F32)
        wk2 = [ar.take(E, F32) for _ in range(2)]
        mk = ar.take(E, F32)
        cr = ar.take(E, F32)
        vg_bR = ct["v_g"]
        R_ON = KSUB >= 2

        @block.tensor
        def _(t):
            for T in range(NT):
                if not R_ON:
                    break
                t.wait_ge(sm["d_w"], 64)
                t.wait_ge(sm["v_xn"], 16 * T + 16)
                if T > 0:
                    t.wait_ge(sm["v_cb"], T)
                for k in range(KC):
                    ins = t.matmul(p_c[:, 0:E], xn[:, k * TOK + T * 128: k * TOK + (T + 1) * 128],
                                   rw_sb[:, k * E:(k + 1) * E], start=(k == 0), stop=(k == KC - 1))
                c.inc(ins, "t_rt", 1)
                t.wait_ge(sm["v_cb"], T + 1)
                if T > 0:
                    t.wait_ge(sm["v_cf"], T)
                c.inc(t.transpose(p_w1[0:16, :], comb_tm[:, T * E:(T + 1) * E], ident_f[:]), "t_ct", 1)

        @block.scalar
        def _(s):
            for T in range(NT):
                if not R_ON:
                    break
                s.wait_ge(sm["v_nm"], NT + T + 1)
                c.inc(s.activation(probs, p_c[:, 0:E], AF.Exp, bias=stat[:, 1:2]), "s_lq", 1)

        @block.vector
        def _(v):
            for T in range(NT):
                if not R_ON:
                    break
                v.wait_ge(sm["t_rt"], T + 1)
                def G(inst):
                    gvv = c.inc(inst, "v_g", 1)
                    v.wait_ge(sm["v_g"], gvv)
                G(v.reduce_max(stat[:, 0:1], p_c[:, 0:E], axis=AX))
                c.inc(v.tensor_scalar(stat[:, 1:2], stat[:, 0:1],
                                      consts[:, C_NEG1:C_NEG1 + 1], None, ALU.mult), "v_nm", 1)
                v.wait_ge(sm["s_lq"], T + 1)
                G(v.reduce_sum(stat[:, 2:3], probs, axis=AX))
                G(v.reciprocal(stat[:, 3:4], stat[:, 2:3]))
                G(v.tensor_scalar(pn, probs, stat[:, 3:4], None, ALU.mult))
                G(v.tensor_copy(wk2[0], pn))
                for j in range(3):
                    G(v.reduce_max(stat[:, 0:1], wk2[j % 2], axis=AX))
                    G(v.tensor_single_scalar(mk, wk2[j % 2], stat[:, 0:1], ALU.is_ge))
                    G(v.scalar_tensor_tensor(wk2[(j + 1) % 2], mk, consts[:, C_NBIG:C_NBIG + 1],
                                             wk2[j % 2], ALU.mult, ALU.add))
                G(v.reduce_max(stat[:, 0:1], wk2[1], axis=AX))
                G(v.tensor_single_scalar(mk, pn, stat[:, 0:1], ALU.is_ge))
                G(v.tensor_mul(cr, mk, pn))
                G(v.reduce_sum(stat[:, 1:2], cr, axis=AX))
                G(v.tensor_scalar(stat[:, 4:5], stat[:, 1:2], consts[:, C_TINY:C_TINY + 1], None, ALU.max))
                G(v.reciprocal(stat[:, 2:3], stat[:, 4:5]))
                c.inc(v.tensor_scalar(comb_tm[:, T * E:(T + 1) * E], cr, stat[:, 2:3], None, ALU.mult), "v_cb", 1)
                v.wait_ge(sm["t_ct"], T + 1)
                c.inc(v.tensor_copy(comb_fm[:, T * 128:(T + 1) * 128], p_w1[0:16, :]), "v_cf", 1)

        if max_phase >= 1:
            ar.reset()
            cos_f = ar.take(TOK, F32)
            sin_f = ar.take(TOK, F32)
            wproj = [ar.take(KC * 128, BF16) for _ in range(2)]
            wvproj = [ar.take(KC * 512, BF16) for _ in range(2)]
            kbf = ar.take(512, BF16)
            kra = ar.take(512, F32)
            krb = ar.take(512, F32)
            krope_f = [ar.take(512, F32) for _ in range(2)]
            krope = [ar.take(512, BF16) for _ in range(2)]
            ktmT = [ar.take(512, BF16) for _ in range(2)]
            vtm = [ar.take(512, BF16) for _ in range(2)]

            @block.sync
            def _(sy):
                sy.wait_ge(sm["s_xn"], NT)
                c.dma(sy, "d_cs", cos_f, cos_in.ap())
                c.dma(sy, "d_cs", sin_f, sin_in.ap())


            @block.gpsimd
            def _(g):
                g.wait_ge(sm["v_cf"], NT)
                for h in range(H):
                    if h >= 2:
                        g.wait_ge(sm["t_k"], (h - 1) * NTG)
                    c.dma(g, f"d_wk{h % 2}", wproj[h % 2].rearrange("p (k m) -> p k m", k=KC),
                          wk_t.ap()[:, h * 128:(h + 1) * 128].rearrange("(k p) m -> p k m", p=128))
                for eg in range(NTG):
                    if eg >= 2:
                        g.wait_ge(sm["t_v"], (eg - 1) * NT)
                    c.dma(g, f"d_wv{eg % 2}", wvproj[eg % 2].rearrange("p (k m) -> p k m", k=KC),
                          wv_t.ap()[:, eg * 512:(eg + 1) * 512].rearrange("(k p) m -> p k m", p=128))

            @block.tensor
            def _(t):
                t.wait_ge(sm["v_xn"], 16 * NT)
                t.wait_ge(sm["d_cs"], 32)
                t.wait_ge(sm["v_cf"], NT)
                for h in range(H):
                    t.wait_ge(sm[f"d_wk{h % 2}"], 16 * (h // 2 + 1))
                    for tg in range(NTG):
                        idx = h * NTG + tg
                        t.wait_ge(sm["v_k1"], idx)
                        for k in range(KC):
                            ins = t.matmul(p_a[:, :], wproj[h % 2][:, k * 128:(k + 1) * 128],
                                           xn[:, k * TOK + tg * 512: k * TOK + (tg + 1) * 512],
                                           start=(k == 0), stop=(k == KC - 1))
                        c.inc(ins, "t_k", 1)
                        t.wait_ge(sm["v_k1"], idx + 1)
                        t.wait_ge(sm["v_k2"], idx)
                        c.inc(t.matmul(p_b[:, :], ropeP[:], kbf, start=True, stop=True), "t_pm", 1)
                        t.wait_ge(sm["v_k2"], idx + 1)
                        for j in range(4):
                            tj = idx * 4 + j
                            if tj >= 2:
                                t.wait_ge(sm["v_k3"], tj - 1)
                            c.inc(t.transpose(trb[tj % 2][:, :],
                                              krope_f[idx % 2][:, j * 128:(j + 1) * 128], ident_f[:]),
                                  "t_tk", 1)
                for eg in range(NTG):
                    t.wait_ge(sm[f"d_wv{eg % 2}"], 16 * (eg // 2 + 1))
                    for T in range(NT):
                        idxv = eg * NT + T
                        if idxv == 0:
                            t.wait_ge(sm["v_k1"], H * NTG)
                        t.wait_ge(sm["v_v"], idxv)
                        for k in range(KC):
                            ins = t.matmul(p_a[:, :], xn[:, k * TOK + T * 128: k * TOK + (T + 1) * 128],
                                           wvproj[eg % 2][:, k * 512:(k + 1) * 512],
                                           start=(k == 0), stop=(k == KC - 1))
                        c.inc(ins, "t_v", 1)

            @block.vector
            def _(v):
                v.wait_ge(sm["d_cs"], 32)
                for h in range(H):
                    for tg in range(NTG):
                        idx = h * NTG + tg
                        v.wait_ge(sm["t_k"], idx + 1)
                        c.inc(v.tensor_copy(kbf, p_a[:, :]), "v_k1", 1)
                        v.wait_ge(sm["t_pm"], idx + 1)
                        if idx >= 2:
                            v.wait_ge(sm[f"d_kr{idx % 2}"], 16 * (idx // 2))
                            v.wait_ge(sm["t_tk"], 4 * (idx - 1))
                        v.tensor_mul(kra, kbf, cos_f[:, tg * 512:(tg + 1) * 512])
                        v.tensor_mul(krb, p_b[:, :], sin_f[:, tg * 512:(tg + 1) * 512])
                        v.tensor_add(krope_f[idx % 2], kra, krb)
                        c.inc(v.tensor_copy(krope[idx % 2], krope_f[idx % 2]), "v_k2", 1)
                        if idx >= 2:
                            v.wait_ge(sm[f"d_kt{idx % 2}"], 64 * (idx // 2))
                        for j in range(4):
                            tj = idx * 4 + j
                            v.wait_ge(sm["t_tk"], tj + 1)
                            c.inc(v.tensor_copy(ktmT[idx % 2][:, j * 128:(j + 1) * 128],
                                                trb[tj % 2][:, :]), "v_k3", 1)
                for eg in range(NTG):
                    for T in range(NT):
                        idxv = eg * NT + T
                        v.wait_ge(sm["t_v"], idxv + 1)
                        if idxv >= 2:
                            v.wait_ge(sm[f"d_vt{idxv % 2}"], 64 * (idxv // 2))
                        c.inc(v.tensor_scalar(vtm[idxv % 2], p_a[:, :],
                                              consts[:, C_INVN:C_INVN + 1], None, ALU.mult), "v_v", 1)

            @block.sync
            def _(sy):
                for h in range(H):
                    for tg in range(NTG):
                        idx = h * NTG + tg
                        sy.wait_ge(sm["v_k2"], idx + 1)
                        c.dma(sy, f"d_kr{idx % 2}", kfm_l.ap()[h, :, tg * 512:(tg + 1) * 512],
                              krope[idx % 2])
                        sy.wait_ge(sm["v_k3"], 4 * (idx + 1))
                        for sub in range(4):
                            c0 = tg * 32 + sub * 8
                            c.dma(sy, f"d_kt{idx % 2}",
                                  ktm_l.ap()[h, :, c0:c0 + 8, :].rearrange("p c d -> c p d"),
                                  ktmT[idx % 2][:, sub * 128:(sub + 1) * 128])
                for eg in range(NTG):
                    for T in range(NT):
                        idxv = eg * NT + T
                        sy.wait_ge(sm["v_v"], idxv + 1)
                        for hh in range(4):
                            c.dma(sy, f"d_vt{idxv % 2}",
                                  vtm_l.ap()[4 * eg + hh, :, T * 8:(T + 1) * 8, :].rearrange("p c e -> c p e"),
                                  vtm[idxv % 2][:, hh * 128:(hh + 1) * 128])

        if max_phase >= 2:
            @block.gpsimd
            def _(g):
                for nm in ("d_kr0", "d_kr1", "d_kt0", "d_kt1", "d_vt0", "d_vt1"):
                    g.wait_ge(sm[nm], ct[nm])
                c.inc(g.collective_compute(
                    "AllGather", ALU.bypass, replica_groups=[list(range(8))],
                    ins=[kfm_l.ap().rearrange("h d t -> (h d) t")],
                    outs=[kfm_g.ap().rearrange("r h d t -> (r h d) t")]), "cc", 1)
                c.inc(g.collective_compute(
                    "AllGather", ALU.bypass, replica_groups=[list(range(8))],
                    ins=[ktm_l.ap().rearrange("h p c d -> (h p) (c d)")],
                    outs=[ktm_g.ap().rearrange("r h p c d -> (r h p) (c d)")]), "cc", 1)
                c.inc(g.collective_compute(
                    "AllGather", ALU.bypass, replica_groups=[list(range(8))],
                    ins=[vtm_l.ap().rearrange("h p c d -> (h p) (c d)")],
                    outs=[vtm_g.ap().rearrange("r h p c d -> (r h p) (c d)")]), "cc", 1)

        if max_phase >= 3:
            ar.reset()
            kfm_sb = [[ar.take(TOK, BF16) for _ in range(2)] for _ in range(2)]
            kcs_sb = [[ar.take(16 * 128, BF16)[0:16, :] for _ in range(2)] for _ in range(2)]
            vcs_sb = [[ar.take(16 * 128, BF16)[0:16, :] for _ in range(2)] for _ in range(2)]
            z_sub = [[ar.take(16 * 128, BF16)[0:16, :] for _ in range(2)] for _ in range(2)]
            dz_sb = [ar.take(128, BF16)[0:16, :] for _ in range(2)]
            wsb = [ar.take(128, BF16) for _ in range(2)]
            wf = [[ar.take(128, F32) for _ in range(2)] for _ in range(2)]
            zb = [p_z, p_a]
            ninv16 = consts[0:16, C_NINV:C_NINV + 1]

            for half in range(2):
                tz_b = {s: ct[f"t_z{s}"] for s in range(2)}
                tg_b = {s: ct[f"t_g{s}"] for s in range(2)}
                vd_b = {s: ct[f"v_dz{s}"] for s in range(2)}
                sw_b = {s: ct[f"s_w{s}"] for s in range(2)}
                vw_b = {s: ct[f"v_w{s}"] for s in range(2)}
                zc_b = {s: ct[f"s_zc{s}"] for s in range(2)}
                zwb_sl = {(s, j): ct[f"zw{s}{j}"] for s in range(2) for j in range(2)}
                kfm_cnt, kcs_cnt, vcs_cnt = {}, {}, {}

                @block.gpsimd
                def _(g, half=half, tz_b=tz_b, tg_b=tg_b, vd_b=vd_b):
                    if half == 0:
                        g.wait_ge(sm["cc"], 3)
                        g.wait_ge(sm["v_v"], ct["v_v"])

                    def body(lc):
                        b_, q_ = lc // 4, lc % 4
                        for ss in range((KSCAN + 15) // 16):
                            r = ss // 8
                            if ss % 8 == 0:
                                for s in range(2):
                                    hl = half * 2 + s
                                    h_ = 4 * q_ + hl
                                    g.wait_ge(sm[f"t_z{s}"], tz_b[s] + max(0, r - 1) * CPR)
                                    kfm_cnt[(s, r)] = c.dma(
                                        g, f"kf{s}{r % 2}", kfm_sb[s][r % 2],
                                        kfm_g.ap()[4 * b_ + r, h_, :, :])
                            for s in range(2):
                                hl = half * 2 + s
                                h_ = 4 * q_ + hl
                                g.wait_ge(sm[f"t_g{s}"], tg_b[s] + ((ss - 1) * 16 if ss >= 2 else 0))
                                kcs_cnt[(s, ss)] = c.dma(
                                    g, f"kc{s}{ss % 2}", kcs_sb[s][ss % 2],
                                    ktm_g.ap()[4 * b_ + r, h_, :, (ss % 8) * 16:(ss % 8) * 16 + 16, :]
                                    .rearrange("p c d -> p (c d)"))
                                g.wait_ge(sm[f"v_dz{s}"], vd_b[s] + ((ss - 1) * 16 if ss >= 2 else 0))
                                vcs_cnt[(s, ss)] = c.dma(
                                    g, f"vc{s}{ss % 2}", vcs_sb[s][ss % 2],
                                    vtm_g.ap()[4 * b_ + r, h_, :, (ss % 8) * 16:(ss % 8) * 16 + 16, :]
                                    .rearrange("p c d -> p (c d)"))
                    c.core_switch(g, body)

                @block.tensor
                def _(t, half=half, tz_b=tz_b, tg_b=tg_b, vd_b=vd_b, sw_b=sw_b, zc_b=zc_b,
                      kfm_cnt=kfm_cnt, kcs_cnt=kcs_cnt):
                    if half == 0:
                        t.wait_ge(sm["v_v"], ct["v_v"])
                    for i in range(KSCAN):
                        r, cl = i // CPR, i % CPR
                        ss, cs = i // 16, i % 16
                        for s in range(2):
                            t.wait_ge(sm[f"s_w{s}"], sw_b[s] + i + 1)
                            t.wait_ge(sm[f"kf{s}{r % 2}"], kfm_cnt[(s, r)])
                            t.wait_ge(sm[f"v_dz{s}"], vd_b[s] + i)
                            if KZ:
                                t.wait_ge(sm[f"s_zc{s}"], zc_b[s] + i)
                            c.inc(t.matmul(zb[s][0:16, 0:128], kfm_sb[s][r % 2][:, cl * 16:(cl + 1) * 16],
                                           wsb[s], start=True, stop=True), f"t_z{s}", 1)
                        for s in range(2):
                            t.wait_ge(sm[f"v_dz{s}"], vd_b[s] + i + 1)
                            t.wait_ge(sm[f"kc{s}{ss % 2}"], kcs_cnt[(s, ss)])
                            t.wait_ge(sm[f"v_w{s}"], vw_b[s] + i)
                            c.inc(t.matmul(p_wb[s][:, :], kcs_sb[s][ss % 2][:, cs * 128:(cs + 1) * 128],
                                           dz_sb[s], start=True, stop=True), f"t_g{s}", 1)

                @block.vector
                def _(v, half=half, tz_b=tz_b, tg_b=tg_b, vd_b=vd_b, vw_b=vw_b, sw_b=sw_b, vcs_cnt=vcs_cnt):
                    if half == 0:
                        for nm in ("d_kt0", "d_kt1", "d_vt0", "d_vt1"):
                            v.wait_ge(sm[nm], ct[nm])
                    for s in range(2):
                        v.tensor_copy(wf[s][0], ident_f[:])
                    for i in range(KSCAN):
                        ss, cs = i // 16, i % 16
                        for s in range(2):
                            v.wait_ge(sm[f"t_z{s}"], tz_b[s] + i + 1)
                            v.wait_ge(sm[f"vc{s}{ss % 2}"], vcs_cnt[(s, ss)])
                            v.wait_ge(sm[f"t_g{s}"], tg_b[s] + i)
                            c.inc(v.scalar_tensor_tensor(dz_sb[s], zb[s][0:16, 0:128], ninv16,
                                                         vcs_sb[s][ss % 2][:, cs * 128:(cs + 1) * 128],
                                                         ALU.mult, ALU.add), f"v_dz{s}", 1)
                        for s in range(2):
                            v.wait_ge(sm[f"t_g{s}"], tg_b[s] + i + 1)
                            v.wait_ge(sm[f"s_w{s}"], sw_b[s] + i)
                            c.inc(v.tensor_add(wf[s][(i + 1) % 2], wf[s][i % 2], p_wb[s][:, :]), f"v_w{s}", 1)

                @block.sync
                def _(sy, half=half, zc_b=zc_b):
                    for ss in range(KSCAN // 16 if KZ == 1 else 0):
                        for s in range(2):
                            hl = half * 2 + s
                            sy.wait_ge(sm[f"s_zc{s}"], zc_b[s] + (ss + 1) * 16)
                            c.dma(sy, f"zw{s}{ss % 2}",
                                  z_l.ap()[hl, :, ss * 16:(ss + 1) * 16, :].rearrange("p c e -> p (c e)"),
                                  z_sub[s][ss % 2])

                @block.scalar
                def _(s_, half=half, tz_b=tz_b, tg_b=tg_b, zc_b=zc_b, sw_b=sw_b,
                      vw_b=vw_b, vd_b=vd_b, zwb_sl=zwb_sl):
                    if half == 0:
                        for nm in ("d_kt0", "d_kt1", "d_vt0", "d_vt1"):
                            s_.wait_ge(sm[nm], ct[nm])
                    for s in range(2):
                        if half == 1:
                            s_.wait_ge(sm[f"t_z{s}"], tz_b[s])
                        c.inc(s_.copy(wsb[s], ident_b[:]), f"s_w{s}", 1)
                    for i in range(KSCAN):
                        ss, cs = i // 16, i % 16
                        for s in range(2):
                            hl = half * 2 + s
                            if KZ:
                                s_.wait_ge(sm[f"t_z{s}"], tz_b[s] + i + 1)
                                s_.wait_ge(sm[f"v_dz{s}"], vd_b[s] + i + 1)
                                if cs == 0 and ss >= 2 and KZ == 1:
                                    s_.wait_ge(sm[f"zw{s}{ss % 2}"], zwb_sl[(s, ss % 2)] + 16 * (ss // 2))
                                c.inc(s_.copy(z_sub[s][ss % 2][:, cs * 128:(cs + 1) * 128],
                                              zb[s][0:16, 0:128]), f"s_zc{s}", 1)
                            s_.wait_ge(sm[f"v_w{s}"], vw_b[s] + i + 1)
                            c.inc(s_.copy(wsb[s], wf[s][(i + 1) % 2]), f"s_w{s}", 1)

        if max_phase >= 4:
            @block.gpsimd
            def _(g):
                for s in range(2):
                    for j in range(2):
                        g.wait_ge(sm[f"zw{s}{j}"], ct[f"zw{s}{j}"])
                c.inc(g.collective_compute(
                    "AllGather", ALU.bypass, replica_groups=[list(range(8))],
                    ins=[z_l.ap().rearrange("l p c e -> (l p) (c e)")],
                    outs=[z_gt.ap().rearrange("r l p c e -> (r l p) (c e)")]), "cc", 1)

        if max_phase >= 5:
            ar.reset()
            wgu = [ar.take(KC * 2 * 128, BF16) for _ in range(2)]
            wdn = [ar.take(MC * 128, BF16) for _ in range(2)]
            t_e = ar.take(MC * TOK, BF16)
            silu_g4 = [ar.take(512, F32) for _ in range(4)]
            su = ar.take(512, F32)
            m5 = ar.mark()
            comb_row = ar.take(TOK, F32)[0:1, :]
            comb_sb = ar.take(TOK, F32)
            devac_w = [ar.take(TOK, F32) for _ in range(2)]
            pg4 = [p_a, p_b, p_c, p_f]
            SCAN_DONE = {k: ct[k] for k in ("v_w0", "v_w1", "v_dz0", "v_dz1", "s_zc0", "s_zc1", "s_w0", "s_w1")}

            @block.gpsimd
            def _(g):
                g.wait_ge(sm["v_cf"], NT)
                g.wait_ge(sm["s_w0"], SCAN_DONE["s_w0"])
                g.wait_ge(sm["s_w1"], SCAN_DONE["s_w1"])
                for s in range(2):
                    for j in range(2):
                        g.wait_ge(sm[f"zw{s}{j}"], ct[f"zw{s}{j}"])
                for e in range(E):
                    g.wait_ge(sm["t_bc"], 4 * e)
                    c.dma(g, "d_cr", comb_row, comb_fm[e:e + 1, :])
                    for m in range(MC):
                        gi = e * MC + m
                        if gi >= 2:
                            g.wait_ge(sm["t_mu"], (gi - 1) * 4)
                        vw = wgu[gi % 2].rearrange("p (k g m) -> p k g m", k=KC, g=2)
                        c.dma(g, f"d_gu{gi % 2}", vw[:, :, 0, :],
                              wg_in.ap()[e, :, m * 128:(m + 1) * 128].rearrange("(k p) m -> p k m", p=128))
                        c.dma(g, f"d_gu{gi % 2}", vw[:, :, 1, :],
                              wu_in.ap()[e, :, m * 128:(m + 1) * 128].rearrange("(k p) m -> p k m", p=128))
                    for o in range(E + 1):
                        if o < E:
                            oi = e * E + o
                            if oi >= 2:
                                g.wait_ge(sm["t_dn"], (oi - 1) * 4)
                            c.dma(g, f"d_dn{oi % 2}", wdn[oi % 2].rearrange("p (k m) -> p k m", k=MC),
                                  wd_in.ap()[e, :, o * 128:(o + 1) * 128].rearrange("(k p) m -> p k m", p=128))
                        if o >= 1:
                            oi = e * E + (o - 1)
                            g.wait_ge(sm["s_ev"], 4 * (oi + 1))
                            c.dma(g, f"d_ac{oi % 2}",
                                  routed_d.ap()[(o - 1) * 128:o * 128, :],
                                  devac_w[oi % 2],
                                  accum_op=(ALU.add if e > 0 else ALU.bypass))

            @block.tensor
            def _(t):
                for k in ("v_w0", "v_w1", "v_dz0", "v_dz1", "s_zc0", "s_zc1"):
                    t.wait_ge(sm[k], SCAN_DONE[k])
                for e in range(E):
                    t.wait_ge(sm["d_cr"], 16 * (e + 1))
                    for tg in range(NTG):
                        if e > 0:
                            t.wait_ge(sm["s_ev"], e * 64)
                        t.wait_ge(sm["v_cs"], 4 * e + tg)
                        c.inc(t.matmul(p_e[:, :], ones_r[:], comb_row[:, tg * 512:(tg + 1) * 512],
                                       start=True, stop=True), "t_bc", 1)
                    for m in range(MC):
                        gi = e * MC + m
                        t.wait_ge(sm[f"d_gu{gi % 2}"], 32 * (gi // 2 + 1))
                        for k in range(KC):
                            for tg in range(NTG):
                                if k == 0 and gi * 4 + tg >= 4:
                                    t.wait_ge(sm["v_su"], gi * 4 + tg - 3)
                                ins = t.matmul(pg4[tg][:, :], wgu[gi % 2][:, (k * 2) * 128:(k * 2 + 1) * 128],
                                               xn[:, k * TOK + tg * 512: k * TOK + (tg + 1) * 512],
                                               start=(k == 0), stop=(k == KC - 1))
                                if k == KC - 1:
                                    c.inc(ins, "t_mg", 1)
                        for k in range(KC):
                            for tg in range(NTG):
                                if k == 0:
                                    t.wait_ge(sm["s_si"], gi * 4 + tg + 1)
                                ins = t.matmul(pg4[tg][:, :], wgu[gi % 2][:, (k * 2 + 1) * 128:(k * 2 + 2) * 128],
                                               xn[:, k * TOK + tg * 512: k * TOK + (tg + 1) * 512],
                                               start=(k == 0), stop=(k == KC - 1))
                                if k == KC - 1:
                                    c.inc(ins, "t_mu", 1)
                    for o in range(E):
                        oi = e * E + o
                        t.wait_ge(sm[f"d_dn{oi % 2}"], 16 * (oi // 2 + 1))
                        if o == 0:
                            t.wait_ge(sm["v_cs"], 4 * (e + 1))
                            t.wait_ge(sm["v_su"], (e + 1) * 44)
                        for k in range(MC):
                            for tg in range(NTG):
                                if k == 0:
                                    t.wait_ge(sm["s_ev"], 4 * oi)
                                ins = t.matmul(pg4[tg][:, :], wdn[oi % 2][:, k * 128:(k + 1) * 128],
                                               t_e[:, k * TOK + tg * 512: k * TOK + (tg + 1) * 512],
                                               start=(k == 0), stop=(k == MC - 1))
                                if k == MC - 1:
                                    c.inc(ins, "t_dn", 1)

            @block.scalar
            def _(s):
                for e in range(E):
                    for m in range(MC):
                        gi = e * MC + m
                        for tg in range(NTG):
                            s.wait_ge(sm["t_mg"], gi * 4 + tg + 1)
                            if gi * 4 + tg >= 4:
                                s.wait_ge(sm["v_su"], gi * 4 + tg - 3)
                            c.inc(s.activation(silu_g4[tg], pg4[tg][:, :], AF.Silu), "s_si", 1)
                    for o in range(E):
                        oi = e * E + o
                        for tg in range(NTG):
                            di = oi * NTG + tg
                            s.wait_ge(sm["t_dn"], oi * 4 + tg + 1)
                            if tg == 0 and oi >= 2:
                                s.wait_ge(sm[f"d_ac{oi % 2}"], 16 * (oi // 2))
                            c.inc(s.copy(devac_w[oi % 2][:, tg * 512:(tg + 1) * 512], pg4[tg][:, :]), "s_ev", 1)

            @block.vector
            def _(v):
                for s in range(2):
                    for j in range(2):
                        v.wait_ge(sm[f"zw{s}{j}"], ct[f"zw{s}{j}"])
                for e in range(E):
                    for tg in range(NTG):
                        v.wait_ge(sm["t_bc"], 4 * e + tg + 1)
                        c.inc(v.tensor_copy(comb_sb[:, tg * 512:(tg + 1) * 512], p_e[:, :]), "v_cs", 1)
                    for m in range(MC):
                        gi = e * MC + m
                        for tg in range(NTG):
                            v.wait_ge(sm["s_si"], gi * 4 + tg + 1)
                            v.wait_ge(sm["t_mu"], gi * 4 + tg + 1)
                            v.tensor_mul(su, silu_g4[tg], pg4[tg][:, :])
                            if m == 0 and tg == 0 and e > 0:
                                v.wait_ge(sm["t_dn"], e * 64)
                            c.inc(v.tensor_mul(t_e[:, m * TOK + tg * 512:m * TOK + (tg + 1) * 512],
                                               su, comb_sb[:, tg * 512:(tg + 1) * 512]), "v_su", 1)

        if max_phase >= 5:
            ar.rollback(m5)
            sg_row = ar.take(TOK, F32)[0:1, :]
            sg_bc = ar.take(TOK, F32)
            shd_big = [ar.take(SHC * 128, BF16) for _ in range(2)]
            devac = [ar.take(512, F32) for _ in range(2)]
            t_sh = t_e
            pga = [p_a, p_c]
            pgu = [p_b, p_f]
            silu_g = silu_g4[:2]
            gu_b = {0: ct["d_gu0"], 1: ct["d_gu1"]}
            dn_b = {0: ct["d_dn0"], 1: ct["d_dn1"]}
            ac_b = {0: ct["d_ac0"], 1: ct["d_ac1"]}
            tdn_b = ct["t_dn"]
            tmg_b = ct["t_mg"]
            tmu_b = ct["t_mu"]
            ssi_b = ct["s_si"]
            vte_b = ct["v_te"]
            VCS_ALL = ct["v_cs"]
            SEV_ALL = ct["s_ev"]
            SSI_ALL = ct["s_si"]
            VSU_ALL = ct["v_su"]

            @block.tensor
            def _(t):
                t.wait_ge(sm["d_w"], 64)
                t.wait_ge(sm["s_ev"], SEV_ALL)
                t.wait_ge(sm["v_cs"], VCS_ALL)
                t.wait_ge(sm["s_si"], SSI_ALL)
                t.wait_ge(sm["v_su"], VSU_ALL)
                for tg in range(NTG):
                    t.wait_ge(sm["v_sg"], 2 * tg)
                    for k in range(KC):
                        ins = t.matmul(p_e[0:1, :], sgw_sb[:, k:k + 1],
                                       xn[:, k * TOK + tg * 512: k * TOK + (tg + 1) * 512],
                                       start=(k == 0), stop=(k == KC - 1))
                    c.inc(ins, "t_sg", 1)
                    t.wait_ge(sm["s_sg"], tg + 1)
                    c.inc(t.matmul(p_e[:, :], ones_r[:], sg_row[:, tg * 512:(tg + 1) * 512],
                                   start=True, stop=True), "t_sg", 1)
                for tg in range(NTG):
                    for ic in range(SHC):
                        gi = tg * SHC + ic
                        t.wait_ge(sm[f"d_gu{gi % 2}"], gu_b[gi % 2] + 32 * (gi // 2 + 1))
                        t.wait_ge(sm["s_si"], ssi_b + max(0, gi - 1))
                        for k in range(KC):
                            ins = t.matmul(pga[gi % 2][:, :], wgu[gi % 2][:, (k * 2) * 128:(k * 2 + 1) * 128],
                                           xn[:, k * TOK + tg * 512: k * TOK + (tg + 1) * 512],
                                           start=(k == 0), stop=(k == KC - 1))
                        c.inc(ins, "t_mg", 1)
                        t.wait_ge(sm["v_te"], vte_b + max(0, gi - 1))
                        for k in range(KC):
                            ins = t.matmul(pgu[gi % 2][:, :], wgu[gi % 2][:, (k * 2 + 1) * 128:(k * 2 + 2) * 128],
                                           xn[:, k * TOK + tg * 512: k * TOK + (tg + 1) * 512],
                                           start=(k == 0), stop=(k == KC - 1))
                        c.inc(ins, "t_mu", 1)
                    for o in range(E):
                        oi = tg * E + o
                        t.wait_ge(sm[f"d_dn{oi % 2}"], dn_b[oi % 2] + 16 * (oi // 2 + 1))
                        if oi == 0:
                            t.wait_ge(sm["v_sg"], 8)
                        if o == 0:
                            t.wait_ge(sm["v_te"], vte_b + (tg + 1) * SHC)
                        t.wait_ge(sm["v_sh"], oi)
                        for k in range(SHC):
                            ins = t.matmul(p_e[:, :], shd_big[oi % 2][:, k * 128:(k + 1) * 128],
                                           t_sh[:, k * 512:(k + 1) * 512],
                                           start=(k == 0), stop=(k == SHC - 1))
                        c.inc(ins, "t_dn", 1)

            @block.scalar
            def _(s):
                for tg in range(NTG):
                    s.wait_ge(sm["t_sg"], 2 * tg + 1)
                    c.inc(s.activation(sg_row[:, tg * 512:(tg + 1) * 512], p_e[0:1, :], AF.Sigmoid), "s_sg", 1)
                for tg in range(NTG):
                    for ic in range(SHC):
                        gi = tg * SHC + ic
                        s.wait_ge(sm["t_mg"], tmg_b + gi + 1)
                        if gi >= 2:
                            s.wait_ge(sm["v_te"], vte_b + gi - 1)
                        c.inc(s.activation(silu_g[gi % 2], pga[gi % 2][:, :], AF.Silu), "s_si", 1)

            @block.vector
            def _(v):
                for tg in range(NTG):
                    v.wait_ge(sm["t_sg"], 2 * tg + 2)
                    c.inc(v.tensor_copy(sg_bc[:, tg * 512:(tg + 1) * 512], p_e[:, :]), "v_sg", 2)
                for tg in range(NTG):
                    for ic in range(SHC):
                        gi = tg * SHC + ic
                        v.wait_ge(sm["s_si"], ssi_b + gi + 1)
                        v.wait_ge(sm["t_mu"], tmu_b + gi + 1)
                        if ic == 0 and tg > 0:
                            v.wait_ge(sm["t_dn"], tdn_b + tg * E)
                        c.inc(v.tensor_mul(t_sh[:, ic * 512:(ic + 1) * 512], silu_g[gi % 2],
                                           pgu[gi % 2][:, :]), "v_te", 1)
                    for o in range(E):
                        oi = tg * E + o
                        v.wait_ge(sm["t_dn"], tdn_b + oi + 1)
                        if oi >= 2:
                            v.wait_ge(sm[f"d_ac{oi % 2}"], ac_b[oi % 2] + 16 * (oi // 2))
                        c.inc(v.tensor_mul(devac[oi % 2], p_e[:, :],
                                           sg_bc[:, tg * 512:(tg + 1) * 512]), "v_sh", 1)

            @block.gpsimd
            def _(g):
                g.wait_ge(sm["d_ac0"], ac_b[0])
                g.wait_ge(sm["d_ac1"], ac_b[1])
                for tg in range(NTG):
                    for ic in range(SHC):
                        gi = tg * SHC + ic
                        g.wait_ge(sm["t_mu"], tmu_b + max(0, gi - 1))
                        vw = wgu[gi % 2].rearrange("p (k g m) -> p k g m", k=KC, g=2)
                        c.dma(g, f"d_gu{gi % 2}", vw[:, :, 0, :],
                              shg_in.ap()[:, ic * 128:(ic + 1) * 128].rearrange("(k p) m -> p k m", p=128))
                        c.dma(g, f"d_gu{gi % 2}", vw[:, :, 1, :],
                              shu_in.ap()[:, ic * 128:(ic + 1) * 128].rearrange("(k p) m -> p k m", p=128))
                    for o in range(E):
                        oi = tg * E + o
                        if oi >= 2:
                            g.wait_ge(sm["t_dn"], tdn_b + oi - 1)
                        c.dma(g, f"d_dn{oi % 2}", shd_big[oi % 2].rearrange("p (k m) -> p k m", k=SHC),
                              shd_in.ap()[:, o * 128:(o + 1) * 128].rearrange("(k p) m -> p k m", p=128))
                        g.wait_ge(sm["v_sh"], oi + 1)
                        c.dma(g, f"d_ac{oi % 2}",
                              routed_d.ap()[o * 128:(o + 1) * 128, tg * 512:(tg + 1) * 512],
                              devac[oi % 2], accum_op=ALU.add)

        if max_phase >= 6:
            ar.reset()
            gam_b = ar.take(D, F32)
            bet_b = ar.take(D, F32)
            z_tile = ar.take(D, BF16)
            lnsc = ar.take(D, F32)
            lnz_f = ar.take(D, F32)
            lnfin = ar.take(D, F32)
            lnz_fm = ar.take(4 * D, BF16)
            wo_sb = [ar.take(KC * 128, BF16) for _ in range(2)]
            rsl = [ar.take(512, F32) for _ in range(2)]
            asm1 = ar.take(512, F32)
            out_tm = [ar.take(D, F32) for _ in range(4)]
            p_zf = [p_w1[:, :], p_c[:, 0:128]]
            pab = [p_a, p_b]

            b6 = {k: ct[k] for k in ("s_sq", "v_st", "s_sd", "v_nm", "s_xn",
                                     "t_tr", "v_xn", "d_zl", "v_g")}
            AC0, AC1 = ct["d_ac0"], ct["d_ac1"]

            @block.sync
            def _(sy):
                sy.wait_ge(sm["d_ac0"], AC0)
                sy.wait_ge(sm["d_ac1"], AC1)
                c.dma(sy, "d_gb", gam_b, gam_in.ap())
                c.dma(sy, "d_gb", bet_b, bet_in.ap())
                for tg in range(NTG):
                    for o in range(E):
                        woi = tg * E + o
                        if woi >= 2:
                            sy.wait_ge(sm["v_as"], woi - 1)
                        c.dma(sy, f"d_rs{woi % 2}", rsl[woi % 2],
                              routed_d.ap()[o * 128:(o + 1) * 128, tg * 512:(tg + 1) * 512])


            @block.gpsimd
            def _(g):
                g.wait_ge(sm["cc"], ct["cc"])
                g.wait_ge(sm["d_ac0"], AC0)
                g.wait_ge(sm["d_ac1"], AC1)
                for T in range(NT):
                    c.dma(g, "d_out", out_t.ap()[T * 128:(T + 1) * 128, :],
                          x_in.ap()[T * 128:(T + 1) * 128, :])
                for tg in range(NTG):
                    for jj in range(4):
                        T = tg * 4 + jj
                        if T > 0:
                            g.wait_ge(sm["s_xn"], b6["s_xn"] + T)
                        g.wait_ge(sm["d_zl"], b6["d_zl"] + T * 256)

                        def bodyz(lc, T=T):
                            b_, q_ = lc // 4, lc % 4
                            c0 = q_ * CPR + T * 8
                            for rh in range(4):
                                for hl in range(HL):
                                    c.inc(g.dma_start(
                                        out=z_tile[:, (rh * 4 + hl) * 128:(rh * 4 + hl + 1) * 128],
                                        in_=z_gt.ap()[4 * b_ + rh, hl, :, c0:c0 + 8, :]
                                        .rearrange("p c e -> c p e")), "d_zl", 16)
                        c.core_switch(g, bodyz)
                    for o in range(E):
                        woi = tg * E + o
                        if woi >= 2:
                            g.wait_ge(sm["t_wo"], woi - 1)
                        c.dma(g, f"d_wo{woi % 2}", wo_sb[woi % 2].rearrange("p (k m) -> p k m", k=KC),
                              wo_t.ap()[:, o * 128:(o + 1) * 128].rearrange("(k p) m -> p k m", p=128))
                    for jj in range(4):
                        T = tg * 4 + jj
                        g.wait_ge(sm["v_ot"], 64 * (tg + 1))
                        c.dma(g, "d_out", out_t.ap()[T * 128:(T + 1) * 128, :],
                              out_tm[jj], accum_op=ALU.add)

            @block.scalar
            def _(s):
                for T in range(NT):
                    s.wait_ge(sm["d_zl"], b6["d_zl"] + (T + 1) * 256)
                    c.inc(s.square(lnsc, z_tile), "s_sq", 1)
                    s.wait_ge(sm["v_st"], b6["v_st"] + T + 1)
                    c.inc(s.activation(stat[:, 4:5], stat[:, 6:7], AF.Sqrt,
                                       bias=consts[:, C_EPS:C_EPS + 1]), "s_sd", 1)
                    s.wait_ge(sm["v_nm"], b6["v_nm"] + T + 1)
                    c.inc(s.activation(lnsc, z_tile, AF.Identity,
                                       bias=stat[:, 6:7], scale=stat[:, 5:6]), "s_xn", 1)

            @block.vector
            def _(v):
                v.wait_ge(sm["d_gb"], 32)
                for tg in range(NTG):
                    for jj in range(4):
                        T = tg * 4 + jj
                        v.wait_ge(sm["d_zl"], b6["d_zl"] + (T + 1) * 256)
                        def G(inst):
                            gvv = c.inc(inst, "v_g", 1)
                            v.wait_ge(sm["v_g"], gvv)
                        G(v.reduce_sum(stat[:, 0:1], z_tile, axis=AX))
                        v.wait_ge(sm["s_sq"], b6["s_sq"] + T + 1)
                        G(v.reduce_sum(stat[:, 1:2], lnsc, axis=AX))
                        G(v.tensor_scalar(stat[:, 2:3], stat[:, 0:1], consts[:, C_INVD:C_INVD + 1], None, ALU.mult))
                        G(v.tensor_scalar(stat[:, 3:4], stat[:, 1:2], consts[:, C_INVD:C_INVD + 1], None, ALU.mult))
                        G(v.tensor_mul(stat[:, 7:8], stat[:, 2:3], stat[:, 2:3]))
                        c.inc(v.tensor_sub(stat[:, 6:7], stat[:, 3:4], stat[:, 7:8]), "v_st", 1)
                        v.wait_ge(sm["s_sd"], b6["s_sd"] + T + 1)
                        G(v.reciprocal(stat[:, 5:6], stat[:, 4:5]))
                        c.inc(v.tensor_scalar(stat[:, 6:7], stat[:, 2:3], stat[:, 5:6],
                                              consts[:, C_NEG1:C_NEG1 + 1], ALU.mult, ALU.mult), "v_nm", 1)
                        v.wait_ge(sm["s_xn"], b6["s_xn"] + T + 1)
                        if T >= 1:
                            v.wait_ge(sm["t_tr"], b6["t_tr"] + 16 * T)
                        v.tensor_mul(lnz_f, lnsc, gam_b)
                        c.inc(v.tensor_add(lnfin, lnz_f, bet_b), "v_g", 1)
                        for k in range(KC):
                            v.wait_ge(sm["t_tr"], b6["t_tr"] + 16 * T + k + 1)
                            dst = lnz_fm[:, k * 512 + jj * 128: k * 512 + (jj + 1) * 128]
                            c.inc(v.tensor_copy(dst, trb[k % 2][:, :]), "v_xn", 1)
                    for o in range(E):
                        woi = tg * E + o
                        if o == 0 and tg > 0:
                            v.wait_ge(sm["d_out"], 16 * NT + 64 * tg)
                        v.wait_ge(sm["t_wo"], woi + 1)
                        v.wait_ge(sm[f"d_rs{woi % 2}"], 16 * (woi // 2 + 1))
                        c.inc(v.tensor_add(asm1, pab[woi % 2][:, :], rsl[woi % 2]), "v_as", 1)
                        for jj in range(4):
                            v.wait_ge(sm["t_ft"], 4 * woi + jj + 1)
                            c.inc(v.tensor_copy(out_tm[jj][:, o * 128:(o + 1) * 128],
                                                p_zf[jj % 2]), "v_ot", 1)
                    for jj in range(4):
                        c.inc(v.engine_nop(), "v_ox", 1)

            @block.tensor
            def _(t):
                t.wait_ge(sm["s_ev"], ct["s_ev"])
                t.wait_ge(sm["v_sh"], ct["v_sh"])
                for tg in range(NTG):
                    for jj in range(4):
                        T = tg * 4 + jj
                        t.wait_ge(sm["v_g"], b6["v_g"] + 7 * T + 7)
                        for k in range(KC):
                            idx = 16 * T + k
                            if idx >= 1:
                                t.wait_ge(sm["v_xn"], b6["v_xn"] + idx - 1)
                            c.inc(t.transpose(trb[idx % 2][:, :],
                                              lnfin[:, k * 128:(k + 1) * 128], ident_f[:]), "t_tr", 1)
                    for o in range(E):
                        woi = tg * E + o
                        t.wait_ge(sm[f"d_wo{woi % 2}"], 16 * (woi // 2 + 1))
                        t.wait_ge(sm["v_as"], max(0, woi - 1))
                        if o == 0:
                            t.wait_ge(sm["v_xn"], b6["v_xn"] + 64 * (tg + 1))
                        for k in range(KC):
                            ins = t.matmul(pab[woi % 2][:, :], wo_sb[woi % 2][:, k * 128:(k + 1) * 128],
                                           lnz_fm[:, k * 512:(k + 1) * 512],
                                           start=(k == 0), stop=(k == KC - 1))
                        c.inc(ins, "t_wo", 1)
                        t.wait_ge(sm["v_as"], woi + 1)
                        for jj in range(4):
                            t.wait_ge(sm["v_ot"], max(0, 4 * woi + jj - 1))
                            c.inc(t.transpose(p_zf[jj % 2], asm1[:, jj * 128:(jj + 1) * 128],
                                              ident_f[:]), "t_ft", 1)

        if max_phase < 6:
            @block.sync
            def _(sy):
                sy.wait_ge(sm["v_xn"], ct["v_xn"])
                for T in range(NT):
                    c.dma(sy, "d_out", out_t.ap()[T * 128:(T + 1) * 128, :],
                          xn[:, 0:4096].bitcast(F32))

        if debug:
            @block.gpsimd
            def _(g):
                g.wait_ge(sm["v_xn"], ct["v_xn"])
                c.dma(g, "d_dbg", dbg["xn"].ap(), xn[:])
                g.wait_ge(sm["v_cf"], ct["v_cf"])
                c.dma(g, "d_dbg", dbg["comb"].ap(), comb_fm[:])
                if max_phase >= 1:
                    for nm in ("d_kr0", "d_kr1", "d_kt0", "d_kt1", "d_vt0", "d_vt1"):
                        g.wait_ge(sm[nm], ct[nm])
                    c.dma(g, "d_dbg", dbg["kfm"].ap(), kfm_l.ap())
                    c.dma(g, "d_dbg", dbg["ktm"].ap(), ktm_l.ap())
                    c.dma(g, "d_dbg", dbg["vtm"].ap(), vtm_l.ap())
                if max_phase >= 3:
                    for s in range(2):
                        for j in range(2):
                            g.wait_ge(sm[f"zw{s}{j}"], ct[f"zw{s}{j}"])
                    c.dma(g, "d_dbg", dbg["zl"].ap(), z_l.ap())
                if max_phase >= 5:
                    g.wait_ge(sm["d_ac0"], ct["d_ac0"])
                    g.wait_ge(sm["d_ac1"], ct["d_ac1"])
                    c.dma(g, "d_dbg", dbg["routed"].ap(), routed_d.ap())

        @block.sync
        def _(sy):
            sy.wait_ge(sm["d_out"], ct["d_out"])
            sy.wait_ge(sm["d_dbg"], ct["d_dbg"])

    return nc


def make_idx_table(core):
    b, q = core // 4, core % 4
    t = np.zeros((128, N_IDX), np.int32)
    for r in range(4):
        for hl in range(HL):
            h = 4 * q + hl
            t[:, IXC_KFM + r * 4 + hl] = ((4 * b + r) * H + h) * 128 + np.arange(128)
            t[0:16, IXC_KTM + r * 4 + hl] = ((4 * b + r) * H + h) * 16 + np.arange(16)
    for T in range(NT):
        for rh in range(4):
            for hl in range(HL):
                col = IXC_Z + T * 16 + rh * 4 + hl
                cc = np.arange(128) // 16
                pp = np.arange(128) % 16
                cglob = q * CPR + T * 8 + cc
                t[:, col] = (((4 * b + rh) * HL + hl) * 16 + pp) * NCHUNK + cglob
    return t


def host_inputs(inputs):
    x = np.asarray(inputs["L0_attn_x"], np.float32)
    Wk = np.asarray(inputs["L0_attn_Wk"], np.float32)
    Wv = np.asarray(inputs["L0_attn_Wv"], np.float32)
    Wo = np.asarray(inputs["L0_attn_Wo"], np.float32)
    gam = np.asarray(inputs["L0_attn_ln_gamma"], np.float32)
    bet = np.asarray(inputs["L0_attn_ln_beta"], np.float32)
    rw = np.asarray(inputs["L0_ffn_router_w"], np.float32)

    shared = {
        "wk_t": np.ascontiguousarray(Wk.T), "wv_t": np.ascontiguousarray(Wv.T),
        "wo_t": np.ascontiguousarray(Wo.T), "rw_t": np.ascontiguousarray(rw.T),
        "wg_in": np.ascontiguousarray(np.asarray(inputs["L0_ffn_w_gate"], np.float32)),
        "wu_in": np.ascontiguousarray(np.asarray(inputs["L0_ffn_w_up"], np.float32)),
        "wd_in": np.ascontiguousarray(np.asarray(inputs["L0_ffn_w_down"], np.float32)),
        "shg_in": np.ascontiguousarray(np.asarray(inputs["L0_ffn_sh_gate"], np.float32)),
        "shu_in": np.ascontiguousarray(np.asarray(inputs["L0_ffn_sh_up"], np.float32)),
        "shd_in": np.ascontiguousarray(np.asarray(inputs["L0_ffn_sh_down"], np.float32)),
        "sgw_in": np.ascontiguousarray(
            np.asarray(inputs["L0_ffn_sh_gate_w"], np.float32).reshape(KC, 128).T),
        "gam_in": np.ascontiguousarray(np.broadcast_to(gam[None, :], (128, D))),
        "bet_in": np.ascontiguousarray(np.broadcast_to(bet[None, :], (128, D))),
        "ident_in": np.eye(128, dtype=np.float32),
        "ones_in": np.ones((1, 128), np.float32),
    }
    P = np.zeros((128, 128), np.float32)
    P[np.arange(64), np.arange(64) + 64] = -1.0
    P[np.arange(64) + 64, np.arange(64)] = 1.0
    shared["ropeP_in"] = np.ascontiguousarray(P.T)
    cvals = np.zeros(8, np.float32)
    cvals[C_INVD] = 1.0 / D
    cvals[C_EPS] = 1e-5
    cvals[C_NEG1] = -1.0
    cvals[C_TINY] = 1e-9
    cvals[C_NBIG] = -1e9
    cvals[C_ONE] = 1.0
    cvals[C_NINV] = -INV_N
    cvals[C_INVN] = INV_N
    shared["consts_in"] = np.ascontiguousarray(np.broadcast_to(cvals[None, :], (128, 8)))

    inv_freq = 1.0 / (10000.0 ** (np.arange(0, DH, 2, dtype=np.float64) / DH))
    in_maps = []
    for core in range(8):
        b, q = core // 4, core % 4
        pos = q * TOK + np.arange(TOK, dtype=np.float64)
        fr = np.outer(pos, inv_freq)
        emb = np.concatenate([fr, fr], axis=1)
        m = dict(shared)
        m["x_in"] = np.ascontiguousarray(x[b, q * TOK:(q + 1) * TOK])
        m["cos_in"] = np.ascontiguousarray(np.cos(emb).T.astype(np.float32))
        m["sin_in"] = np.ascontiguousarray(np.sin(emb).T.astype(np.float32))
        in_maps.append(m)
    return in_maps


def kernel(**inputs):
    in_maps = host_inputs(inputs)
    nc = build_nc(max_phase=6, debug=False)
    res = run_bass_kernel_spmd(nc, in_maps, core_ids=list(range(8)))
    out = np.empty((B, S, D), np.float32)
    for core in range(8):
        b, q = core // 4, core % 4
        out[b, q * TOK:(q + 1) * TOK] = res.results[core]["out"]
    return out


# version 07
# speedup vs baseline: 1.0593x; 1.0593x over previous
import os
import numpy as np
import ml_dtypes
from contextlib import ExitStack
KSUB = int(os.environ.get("KSUB", "9"))
KSCAN = int(os.environ.get("KSCAN", "512"))
KZ = int(os.environ.get("KZ", "1"))

import concourse.bass as bass
import concourse.mybir as mybir
from concourse.bass_utils import run_bass_kernel_spmd

F32 = mybir.dt.float32
BF16 = mybir.dt.bfloat16
U8 = mybir.dt.uint8
I32 = mybir.dt.int32
AF = mybir.ActivationFunctionType
ALU = mybir.AluOpType
AX = mybir.AxisListType.X
BF = ml_dtypes.bfloat16

B, S, D = 2, 8192, 2048
H, DH, MBL = 16, 128, 16
E, I_MOE, I_SH = 16, 1408, 5632
TOK = 2048
NT = TOK // 128
KC = D // 128
NTG = TOK // 512
MC = I_MOE // 128
SHC = I_SH // 128
NCHUNK = S // MBL
CPR = TOK // MBL
INV_N = 2.0 / (MBL * DH)
HL = 4
NSUB = NCHUNK // 16

C_INVD, C_EPS, C_NEG1, C_TINY, C_NBIG, C_ONE, C_NINV, C_INVN = range(8)

IXC_KFM = 0
IXC_KTM = 16
IXC_Z = 32
N_IDX = 32 + 256


class Ctx:
    def __init__(self, nc, es):
        self.nc, self.es = nc, es
        self.sem, self.cnt = {}, {}

    def sem_new(self, *names):
        for n in names:
            self.sem[n] = self.es.enter_context(self.nc.semaphore(n))
            self.cnt[n] = 0

    def inc(self, inst, name, n=1):
        inst.then_inc(self.sem[name], n)
        self.cnt[name] += n
        return self.cnt[name]

    def core_switch(self, eng, body_fn):
        snap = dict(self.cnt)
        final = None
        for arm in self.nc.switch_core_id(engines=[eng], n=128):
            for k in self.cnt:
                self.cnt[k] = snap[k]
            body_fn(arm.logical % 8)
            final = dict(self.cnt)
        self.cnt.update(final)

    def dma(self, eng, sem_name, out, in_, pre_wait=True, **kw):
        if pre_wait:
            eng.wait_ge(self.sem[sem_name], self.cnt[sem_name])
        return self.inc(eng.dma_start(out=out, in_=in_, **kw), sem_name, 16)


class Arena:
    def __init__(self, t, nbytes):
        self.t, self.nbytes, self.off = t, nbytes, 0

    def reset(self):
        self.off = 0
        return self

    def mark(self):
        return self.off

    def rollback(self, m):
        self.off = m

    def take(self, cols, dtype):
        sz = cols * mybir.dt.size(dtype)
        sz_al = (sz + 31) & ~31
        assert self.off + sz_al <= self.nbytes, f"arena overflow {self.off}+{sz_al}"
        v = self.t[:, self.off:self.off + sz].bitcast(dtype)
        self.off += sz_al
        return v


def build_nc(max_phase=6, debug=False):
    nc = bass.Bass("TRN2", target_bir_lowering=False)
    es = ExitStack()
    c = Ctx(nc, es)
    sm, ct = c.sem, c.cnt

    dT = lambda n, sh, dt=F32, kind="ExternalInput": nc.dram_tensor(n, sh, dt, kind=kind)
    x_in = dT("x_in", [TOK, D])
    wk_t = dT("wk_t", [D, D])
    wv_t = dT("wv_t", [D, D])
    wo_t = dT("wo_t", [D, D])
    rw_t = dT("rw_t", [D, E])
    wg_in = dT("wg_in", [E, D, I_MOE])
    wu_in = dT("wu_in", [E, D, I_MOE])
    wd_in = dT("wd_in", [E, I_MOE, D])
    shg_in = dT("shg_in", [D, I_SH])
    shu_in = dT("shu_in", [D, I_SH])
    shd_in = dT("shd_in", [I_SH, D])
    sgw_in = dT("sgw_in", [128, KC])
    gam_in = dT("gam_in", [128, D])
    bet_in = dT("bet_in", [128, D])
    cos_in = dT("cos_in", [128, TOK])
    sin_in = dT("sin_in", [128, TOK])
    ident_in = dT("ident_in", [128, 128])
    ropeP_in = dT("ropeP_in", [128, 128])
    ones_in = dT("ones_in", [1, 128])
    consts_in = dT("consts_in", [128, 8])
    out_t = dT("out", [TOK, D], F32, kind="ExternalOutput")

    kfm_l = nc.dram_tensor("kfm_l", [H, 128, TOK], BF16)
    ktm_l = nc.dram_tensor("ktm_l", [H, 16, CPR, 128], BF16)
    vtm_l = nc.dram_tensor("vtm_l", [H, 16, CPR, 128], BF16)
    kfm_g = nc.dram_tensor("kfm_g", [4, H, 128, TOK], BF16)
    ktm_g = nc.dram_tensor("ktm_g", [4, H, 16, CPR, 128], BF16)
    vtm_g = nc.dram_tensor("vtm_g", [4, H, 16, CPR, 128], BF16)
    z_l = nc.dram_tensor("z_l", [HL, 16, NCHUNK, 128], BF16)
    z_gt = nc.dram_tensor("z_gt", [4, HL, 16, NCHUNK, 128], BF16)
    routed_d = nc.dram_tensor("routed_d", [D, TOK], F32)

                
    dbg = {}
    if debug:
        dbg["xn"] = dT("dbg_xn", [128, KC * TOK], F32, kind="ExternalOutput")
        dbg["comb"] = dT("dbg_comb", [16, TOK], F32, kind="ExternalOutput")
        if max_phase >= 1:
            dbg["kfm"] = dT("dbg_kfm", [H, 128, TOK], BF16, kind="ExternalOutput")
            dbg["ktm"] = dT("dbg_ktm", [H, 16, CPR, 128], BF16, kind="ExternalOutput")
            dbg["vtm"] = dT("dbg_vtm", [H, 16, CPR, 128], BF16, kind="ExternalOutput")
        if max_phase >= 3:
            dbg["zl"] = dT("dbg_zl", [HL, 16, NCHUNK, 128], BF16, kind="ExternalOutput")
        if max_phase >= 5:
            dbg["routed"] = dT("dbg_routed", [D, TOK], F32, kind="ExternalOutput")

    ARENA_B = 132 * 1024
    arena_t = es.enter_context(nc.sbuf_tensor("arena", [128, ARENA_B], U8))
    ar = Arena(arena_t, ARENA_B)
    Sb = lambda n, sh, dt: es.enter_context(nc.sbuf_tensor(n, sh, dt))
    xn = Sb("xn", [128, KC * TOK], BF16)
    consts = Sb("consts", [128, 8], F32)
    ident_f = Sb("ident_f", [128, 128], F32)
    ident_b = Sb("ident_b", [128, 128], BF16)
    ropeP = Sb("ropeP", [128, 128], BF16)
    ones_r = Sb("ones_r", [1, 128], F32)
    stat = Sb("stat", [128, 8], F32)
    comb_tm = Sb("comb_tm", [128, NT * E], F32)
    comb_fm = Sb("comb_fm", [16, TOK], F32)
    rw_sb = Sb("rw_sb", [128, KC * E], BF16)
    sgw_sb = Sb("sgw_sb", [128, KC], BF16)

    Pb = lambda n, sh, dt: es.enter_context(nc.psum_tensor(n, sh, dt))
    p_a = Pb("p_a", [128, 512], F32)
    p_b = Pb("p_b", [128, 512], F32)
    p_c = Pb("p_c", [128, 512], F32)
    p_e = Pb("p_e", [128, 512], F32)
    p_f = Pb("p_f", [128, 512], F32)
    p_w0 = Pb("p_w0", [128, 128], F32)
    p_w1 = Pb("p_w1", [128, 128], F32)
    p_z = Pb("p_z", [128, 128], F32)
    p_wb = [p_w0, p_w1]
    pdn = [p_c, p_e]
    trb = [p_z, p_w0]

    c.sem_new("d_ms", "d_w", "d_cr",
              "d_zl", "d_xr", "d_out", "cc",
              "s_sq", "v_st", "s_sd", "v_g", "v_nm", "s_xn", "t_tr", "v_xn",
              "t_rt", "v_cb", "t_ct", "v_cf", "s_lq",
              "t_k", "v_k1", "t_pm", "v_k2", "t_tk", "v_k3", "t_v", "v_v",
              "t_z0", "t_z1", "v_dz0", "v_dz1", "s_zc0", "s_zc1",
              "t_g0", "t_g1", "s_w0", "s_w1",
              "t_bc", "v_cs", "t_mg", "t_mu", "s_si", "v_su", "v_te",
              "t_dn", "s_ev", "v_sh",
              "t_sg", "s_sg", "v_sg",
              "t_wo", "v_as", "t_ft", "v_ot", "v_ox", "d_dbg", "v_w0", "v_w1",
              "d_wk0", "d_wk1", "d_wv0", "d_wv1", "d_gu0", "d_gu1",
              "d_dn0", "d_dn1", "d_wo0", "d_wo1", "d_ac0", "d_ac1",
              "kf00", "kf01", "kf10", "kf11",
              "kc00", "kc01", "kc10", "kc11",
              "vc00", "vc01", "vc10", "vc11",
              "d_x0", "d_x1", "d_cs", "d_gb",
              "d_kr0", "d_kr1", "d_kt0", "d_kt1", "d_vt0", "d_vt1",
              "zw00", "zw01", "zw10", "zw11", "d_rs0", "d_rs1")

    with es, nc.Block() as block:
        x_tile = [ar.take(D, F32) for _ in range(2)]
        xsq = ar.take(D, F32)
        xn_tm = [ar.take(D, F32) for _ in range(2)]

        @block.sync
        def _(sy):
            c.dma(sy, "d_ms", consts[:], consts_in.ap())
            c.dma(sy, "d_ms", ident_f[:], ident_in.ap())
            c.dma(sy, "d_ms", ones_r[:], ones_in.ap())
            for T in range(NT):
                if T >= 2:
                    sy.wait_ge(sm["s_xn"], T - 1)
                c.dma(sy, f"d_x{T % 2}", x_tile[T % 2], x_in.ap()[T * 128:(T + 1) * 128, :])

        @block.gpsimd
        def _(g):
            c.dma(g, "d_w", ident_b[:], ident_in.ap())
            c.dma(g, "d_w", ropeP[:], ropeP_in.ap())
            c.dma(g, "d_w", rw_sb[:].rearrange("p (k e) -> p k e", k=KC),
                  rw_t.ap().rearrange("(k p) e -> p k e", p=128))
            c.dma(g, "d_w", sgw_sb[:], sgw_in.ap())

        MS_PRE = ct["d_ms"]

        @block.scalar
        def _(s):
            for T in range(NT):
                s.wait_ge(sm[f"d_x{T % 2}"], 16 * (T // 2 + 1))
                c.inc(s.square(xsq, x_tile[T % 2]), "s_sq", 1)
                s.wait_ge(sm["v_st"], T + 1)
                c.inc(s.activation(stat[:, 4:5], stat[:, 6:7], AF.Sqrt,
                                   bias=consts[:, C_EPS:C_EPS + 1]), "s_sd", 1)
                s.wait_ge(sm["v_nm"], T + 1)
                if T >= 2:
                    s.wait_ge(sm["t_tr"], 16 * (T - 1))
                c.inc(s.activation(xn_tm[T % 2], x_tile[T % 2], AF.Identity,
                                   bias=stat[:, 6:7], scale=stat[:, 5:6]), "s_xn", 1)

        @block.vector
        def _(v):
            v.wait_ge(sm["d_ms"], MS_PRE)
            for T in range(NT):
                v.wait_ge(sm[f"d_x{T % 2}"], 16 * (T // 2 + 1))
                def G(inst):
                    gvv = c.inc(inst, "v_g", 1)
                    v.wait_ge(sm["v_g"], gvv)
                G(v.reduce_sum(stat[:, 0:1], x_tile[T % 2], axis=AX))
                v.wait_ge(sm["s_sq"], T + 1)
                G(v.reduce_sum(stat[:, 1:2], xsq, axis=AX))
                G(v.tensor_scalar(stat[:, 2:3], stat[:, 0:1], consts[:, C_INVD:C_INVD + 1], None, ALU.mult))
                G(v.tensor_scalar(stat[:, 3:4], stat[:, 1:2], consts[:, C_INVD:C_INVD + 1], None, ALU.mult))
                G(v.tensor_mul(stat[:, 7:8], stat[:, 2:3], stat[:, 2:3]))
                c.inc(v.tensor_sub(stat[:, 6:7], stat[:, 3:4], stat[:, 7:8]), "v_st", 1)
                v.wait_ge(sm["s_sd"], T + 1)
                G(v.reciprocal(stat[:, 5:6], stat[:, 4:5]))
                c.inc(v.tensor_scalar(stat[:, 6:7], stat[:, 2:3], stat[:, 5:6],
                                      consts[:, C_NEG1:C_NEG1 + 1], ALU.mult, ALU.mult), "v_nm", 1)
                for k in range(KC):
                    if KSUB < 1:
                        break
                    v.wait_ge(sm["t_tr"], 16 * T + k + 1)
                    c.inc(v.tensor_copy(xn[:, k * TOK + T * 128: k * TOK + (T + 1) * 128],
                                        trb[k % 2][:, :]), "v_xn", 1)

        @block.tensor
        def _(t):
            t.wait_ge(sm["d_ms"], MS_PRE)
            t.wait_ge(sm["d_w"], 64)
            for T in range(NT):
                if KSUB < 1:
                    break
                t.wait_ge(sm["s_xn"], T + 1)
                for k in range(KC):
                    idx = 16 * T + k
                    if idx >= 1:
                        t.wait_ge(sm["v_xn"], idx - 1)
                    c.inc(t.transpose(trb[idx % 2][:, :],
                                      xn_tm[T % 2][:, k * 128:(k + 1) * 128], ident_f[:]), "t_tr", 1)

        probs = ar.take(E, F32)
        pn = ar.take(E, F32)
        wk2 = [ar.take(E, F32) for _ in range(2)]
        mk = ar.take(E, F32)
        cr = ar.take(E, F32)
        vg_bR = ct["v_g"]
        R_ON = KSUB >= 2

        @block.tensor
        def _(t):
            for T in range(NT):
                if not R_ON:
                    break
                t.wait_ge(sm["d_w"], 64)
                t.wait_ge(sm["v_xn"], 16 * T + 16)
                if T > 0:
                    t.wait_ge(sm["v_cb"], T)
                for k in range(KC):
                    ins = t.matmul(p_c[:, 0:E], xn[:, k * TOK + T * 128: k * TOK + (T + 1) * 128],
                                   rw_sb[:, k * E:(k + 1) * E], start=(k == 0), stop=(k == KC - 1))
                c.inc(ins, "t_rt", 1)
                t.wait_ge(sm["v_cb"], T + 1)
                if T > 0:
                    t.wait_ge(sm["v_cf"], T)
                c.inc(t.transpose(p_w1[0:16, :], comb_tm[:, T * E:(T + 1) * E], ident_f[:]), "t_ct", 1)

        @block.scalar
        def _(s):
            for T in range(NT):
                if not R_ON:
                    break
                s.wait_ge(sm["v_nm"], NT + T + 1)
                c.inc(s.activation(probs, p_c[:, 0:E], AF.Exp, bias=stat[:, 1:2]), "s_lq", 1)

        @block.vector
        def _(v):
            for T in range(NT):
                if not R_ON:
                    break
                v.wait_ge(sm["t_rt"], T + 1)
                def G(inst):
                    gvv = c.inc(inst, "v_g", 1)
                    v.wait_ge(sm["v_g"], gvv)
                G(v.reduce_max(stat[:, 0:1], p_c[:, 0:E], axis=AX))
                c.inc(v.tensor_scalar(stat[:, 1:2], stat[:, 0:1],
                                      consts[:, C_NEG1:C_NEG1 + 1], None, ALU.mult), "v_nm", 1)
                v.wait_ge(sm["s_lq"], T + 1)
                G(v.reduce_sum(stat[:, 2:3], probs, axis=AX))
                G(v.reciprocal(stat[:, 3:4], stat[:, 2:3]))
                G(v.tensor_scalar(pn, probs, stat[:, 3:4], None, ALU.mult))
                G(v.tensor_copy(wk2[0], pn))
                for j in range(3):
                    G(v.reduce_max(stat[:, 0:1], wk2[j % 2], axis=AX))
                    G(v.tensor_single_scalar(mk, wk2[j % 2], stat[:, 0:1], ALU.is_ge))
                    G(v.scalar_tensor_tensor(wk2[(j + 1) % 2], mk, consts[:, C_NBIG:C_NBIG + 1],
                                             wk2[j % 2], ALU.mult, ALU.add))
                G(v.reduce_max(stat[:, 0:1], wk2[1], axis=AX))
                G(v.tensor_single_scalar(mk, pn, stat[:, 0:1], ALU.is_ge))
                G(v.tensor_mul(cr, mk, pn))
                G(v.reduce_sum(stat[:, 1:2], cr, axis=AX))
                G(v.tensor_scalar(stat[:, 4:5], stat[:, 1:2], consts[:, C_TINY:C_TINY + 1], None, ALU.max))
                G(v.reciprocal(stat[:, 2:3], stat[:, 4:5]))
                c.inc(v.tensor_scalar(comb_tm[:, T * E:(T + 1) * E], cr, stat[:, 2:3], None, ALU.mult), "v_cb", 1)
                v.wait_ge(sm["t_ct"], T + 1)
                c.inc(v.tensor_copy(comb_fm[:, T * 128:(T + 1) * 128], p_w1[0:16, :]), "v_cf", 1)

        if max_phase >= 1:
            ar.reset()
            cos_f = ar.take(TOK, F32)
            sin_f = ar.take(TOK, F32)
            wproj = [ar.take(KC * 128, BF16) for _ in range(2)]
            wvproj = [ar.take(KC * 512, BF16) for _ in range(2)]
            kbf = ar.take(512, BF16)
            kra = ar.take(512, F32)
            krb = ar.take(512, F32)
            krope_f = [ar.take(512, F32) for _ in range(2)]
            krope = [ar.take(512, BF16) for _ in range(2)]
            ktmT = [ar.take(512, BF16) for _ in range(2)]
            vtm = [ar.take(512, BF16) for _ in range(2)]

            @block.sync
            def _(sy):
                sy.wait_ge(sm["s_xn"], NT)
                c.dma(sy, "d_cs", cos_f, cos_in.ap())
                c.dma(sy, "d_cs", sin_f, sin_in.ap())


            @block.gpsimd
            def _(g):
                g.wait_ge(sm["v_cf"], NT)
                for h in range(H):
                    if h >= 2:
                        g.wait_ge(sm["t_k"], (h - 1) * NTG)
                    c.dma(g, f"d_wk{h % 2}", wproj[h % 2].rearrange("p (k m) -> p k m", k=KC),
                          wk_t.ap()[:, h * 128:(h + 1) * 128].rearrange("(k p) m -> p k m", p=128))
                for eg in range(NTG):
                    if eg >= 2:
                        g.wait_ge(sm["t_v"], (eg - 1) * NT)
                    c.dma(g, f"d_wv{eg % 2}", wvproj[eg % 2].rearrange("p (k m) -> p k m", k=KC),
                          wv_t.ap()[:, eg * 512:(eg + 1) * 512].rearrange("(k p) m -> p k m", p=128))

            @block.tensor
            def _(t):
                t.wait_ge(sm["v_xn"], 16 * NT)
                t.wait_ge(sm["d_cs"], 32)
                t.wait_ge(sm["v_cf"], NT)
                for h in range(H):
                    t.wait_ge(sm[f"d_wk{h % 2}"], 16 * (h // 2 + 1))
                    for tg in range(NTG):
                        idx = h * NTG + tg
                        t.wait_ge(sm["v_k1"], idx)
                        for k in range(KC):
                            ins = t.matmul(p_a[:, :], wproj[h % 2][:, k * 128:(k + 1) * 128],
                                           xn[:, k * TOK + tg * 512: k * TOK + (tg + 1) * 512],
                                           start=(k == 0), stop=(k == KC - 1))
                        c.inc(ins, "t_k", 1)
                        t.wait_ge(sm["v_k1"], idx + 1)
                        t.wait_ge(sm["v_k2"], idx)
                        c.inc(t.matmul(p_b[:, :], ropeP[:], kbf, start=True, stop=True), "t_pm", 1)
                        t.wait_ge(sm["v_k2"], idx + 1)
                        for j in range(4):
                            tj = idx * 4 + j
                            if tj >= 2:
                                t.wait_ge(sm["v_k3"], tj - 1)
                            c.inc(t.transpose(trb[tj % 2][:, :],
                                              krope_f[idx % 2][:, j * 128:(j + 1) * 128], ident_f[:]),
                                  "t_tk", 1)
                for eg in range(NTG):
                    t.wait_ge(sm[f"d_wv{eg % 2}"], 16 * (eg // 2 + 1))
                    for T in range(NT):
                        idxv = eg * NT + T
                        if idxv == 0:
                            t.wait_ge(sm["v_k1"], H * NTG)
                        t.wait_ge(sm["v_v"], idxv)
                        for k in range(KC):
                            ins = t.matmul(p_a[:, :], xn[:, k * TOK + T * 128: k * TOK + (T + 1) * 128],
                                           wvproj[eg % 2][:, k * 512:(k + 1) * 512],
                                           start=(k == 0), stop=(k == KC - 1))
                        c.inc(ins, "t_v", 1)

            @block.vector
            def _(v):
                v.wait_ge(sm["d_cs"], 32)
                for h in range(H):
                    for tg in range(NTG):
                        idx = h * NTG + tg
                        v.wait_ge(sm["t_k"], idx + 1)
                        c.inc(v.tensor_copy(kbf, p_a[:, :]), "v_k1", 1)
                        v.wait_ge(sm["t_pm"], idx + 1)
                        if idx >= 2:
                            v.wait_ge(sm[f"d_kr{idx % 2}"], 16 * (idx // 2))
                            v.wait_ge(sm["t_tk"], 4 * (idx - 1))
                        v.tensor_mul(kra, kbf, cos_f[:, tg * 512:(tg + 1) * 512])
                        v.tensor_mul(krb, p_b[:, :], sin_f[:, tg * 512:(tg + 1) * 512])
                        v.tensor_add(krope_f[idx % 2], kra, krb)
                        c.inc(v.tensor_copy(krope[idx % 2], krope_f[idx % 2]), "v_k2", 1)
                        if idx >= 2:
                            v.wait_ge(sm[f"d_kt{idx % 2}"], 64 * (idx // 2))
                        for j in range(4):
                            tj = idx * 4 + j
                            v.wait_ge(sm["t_tk"], tj + 1)
                            c.inc(v.tensor_copy(ktmT[idx % 2][:, j * 128:(j + 1) * 128],
                                                trb[tj % 2][:, :]), "v_k3", 1)
                for eg in range(NTG):
                    for T in range(NT):
                        idxv = eg * NT + T
                        v.wait_ge(sm["t_v"], idxv + 1)
                        if idxv >= 2:
                            v.wait_ge(sm[f"d_vt{idxv % 2}"], 64 * (idxv // 2))
                        c.inc(v.tensor_scalar(vtm[idxv % 2], p_a[:, :],
                                              consts[:, C_INVN:C_INVN + 1], None, ALU.mult), "v_v", 1)

            @block.sync
            def _(sy):
                for h in range(H):
                    for tg in range(NTG):
                        idx = h * NTG + tg
                        sy.wait_ge(sm["v_k2"], idx + 1)
                        c.dma(sy, f"d_kr{idx % 2}", kfm_l.ap()[h, :, tg * 512:(tg + 1) * 512],
                              krope[idx % 2])
                        sy.wait_ge(sm["v_k3"], 4 * (idx + 1))
                        for sub in range(4):
                            c0 = tg * 32 + sub * 8
                            c.dma(sy, f"d_kt{idx % 2}",
                                  ktm_l.ap()[h, :, c0:c0 + 8, :].rearrange("p c d -> c p d"),
                                  ktmT[idx % 2][:, sub * 128:(sub + 1) * 128])
                for eg in range(NTG):
                    for T in range(NT):
                        idxv = eg * NT + T
                        sy.wait_ge(sm["v_v"], idxv + 1)
                        for hh in range(4):
                            c.dma(sy, f"d_vt{idxv % 2}",
                                  vtm_l.ap()[4 * eg + hh, :, T * 8:(T + 1) * 8, :].rearrange("p c e -> c p e"),
                                  vtm[idxv % 2][:, hh * 128:(hh + 1) * 128])

        if max_phase >= 2:
            @block.gpsimd
            def _(g):
                for nm in ("d_kr0", "d_kr1", "d_kt0", "d_kt1", "d_vt0", "d_vt1"):
                    g.wait_ge(sm[nm], ct[nm])
                c.inc(g.collective_compute(
                    "AllGather", ALU.bypass, replica_groups=[[0, 1, 2, 3], [4, 5, 6, 7]],
                    ins=[kfm_l.ap().rearrange("h d t -> (h d) t")],
                    outs=[kfm_g.ap().rearrange("r h d t -> (r h d) t")]), "cc", 1)
                c.inc(g.collective_compute(
                    "AllGather", ALU.bypass, replica_groups=[[0, 1, 2, 3], [4, 5, 6, 7]],
                    ins=[ktm_l.ap().rearrange("h p c d -> (h p) (c d)")],
                    outs=[ktm_g.ap().rearrange("r h p c d -> (r h p) (c d)")]), "cc", 1)
                c.inc(g.collective_compute(
                    "AllGather", ALU.bypass, replica_groups=[[0, 1, 2, 3], [4, 5, 6, 7]],
                    ins=[vtm_l.ap().rearrange("h p c d -> (h p) (c d)")],
                    outs=[vtm_g.ap().rearrange("r h p c d -> (r h p) (c d)")]), "cc", 1)

        if max_phase >= 3:
            ar.reset()
            kfm_sb = [[ar.take(TOK, BF16) for _ in range(2)] for _ in range(2)]
            kcs_sb = [[ar.take(16 * 128, BF16)[0:16, :] for _ in range(2)] for _ in range(2)]
            vcs_sb = [[ar.take(16 * 128, BF16)[0:16, :] for _ in range(2)] for _ in range(2)]
            z_sub = [[ar.take(16 * 128, BF16)[0:16, :] for _ in range(2)] for _ in range(2)]
            dz_sb = [ar.take(128, BF16)[0:16, :] for _ in range(2)]
            wsb = [ar.take(128, BF16) for _ in range(2)]
            wf = [[ar.take(128, F32) for _ in range(2)] for _ in range(2)]
            zb = [p_z, p_a]
            ninv16 = consts[0:16, C_NINV:C_NINV + 1]

            for half in range(2):
                tz_b = {s: ct[f"t_z{s}"] for s in range(2)}
                tg_b = {s: ct[f"t_g{s}"] for s in range(2)}
                vd_b = {s: ct[f"v_dz{s}"] for s in range(2)}
                sw_b = {s: ct[f"s_w{s}"] for s in range(2)}
                vw_b = {s: ct[f"v_w{s}"] for s in range(2)}
                zc_b = {s: ct[f"s_zc{s}"] for s in range(2)}
                zwb_sl = {(s, j): ct[f"zw{s}{j}"] for s in range(2) for j in range(2)}
                kfm_cnt, kcs_cnt, vcs_cnt = {}, {}, {}

                @block.gpsimd
                def _(g, half=half, tz_b=tz_b, tg_b=tg_b, vd_b=vd_b):
                    if half == 0:
                        g.wait_ge(sm["cc"], 3)
                        g.wait_ge(sm["v_v"], ct["v_v"])

                    def body(lc):
                        b_, q_ = lc // 4, lc % 4
                        for ss in range((KSCAN + 15) // 16):
                            r = ss // 8
                            if ss % 8 == 0:
                                for s in range(2):
                                    hl = half * 2 + s
                                    h_ = 4 * q_ + hl
                                    g.wait_ge(sm[f"t_z{s}"], tz_b[s] + max(0, r - 1) * CPR)
                                    kfm_cnt[(s, r)] = c.dma(
                                        g, f"kf{s}{r % 2}", kfm_sb[s][r % 2],
                                        kfm_g.ap()[r, h_, :, :])
                            for s in range(2):
                                hl = half * 2 + s
                                h_ = 4 * q_ + hl
                                g.wait_ge(sm[f"t_g{s}"], tg_b[s] + ((ss - 1) * 16 if ss >= 2 else 0))
                                kcs_cnt[(s, ss)] = c.dma(
                                    g, f"kc{s}{ss % 2}", kcs_sb[s][ss % 2],
                                    ktm_g.ap()[r, h_, :, (ss % 8) * 16:(ss % 8) * 16 + 16, :]
                                    .rearrange("p c d -> p (c d)"))
                                g.wait_ge(sm[f"v_dz{s}"], vd_b[s] + ((ss - 1) * 16 if ss >= 2 else 0))
                                vcs_cnt[(s, ss)] = c.dma(
                                    g, f"vc{s}{ss % 2}", vcs_sb[s][ss % 2],
                                    vtm_g.ap()[r, h_, :, (ss % 8) * 16:(ss % 8) * 16 + 16, :]
                                    .rearrange("p c d -> p (c d)"))
                    c.core_switch(g, body)

                @block.tensor
                def _(t, half=half, tz_b=tz_b, tg_b=tg_b, vd_b=vd_b, sw_b=sw_b, zc_b=zc_b,
                      kfm_cnt=kfm_cnt, kcs_cnt=kcs_cnt):
                    if half == 0:
                        t.wait_ge(sm["v_v"], ct["v_v"])
                    for i in range(KSCAN):
                        r, cl = i // CPR, i % CPR
                        ss, cs = i // 16, i % 16
                        for s in range(2):
                            t.wait_ge(sm[f"s_w{s}"], sw_b[s] + i + 1)
                            t.wait_ge(sm[f"kf{s}{r % 2}"], kfm_cnt[(s, r)])
                            t.wait_ge(sm[f"v_dz{s}"], vd_b[s] + i)
                            if KZ:
                                t.wait_ge(sm[f"s_zc{s}"], zc_b[s] + i)
                            c.inc(t.matmul(zb[s][0:16, 0:128], kfm_sb[s][r % 2][:, cl * 16:(cl + 1) * 16],
                                           wsb[s], start=True, stop=True), f"t_z{s}", 1)
                        for s in range(2):
                            t.wait_ge(sm[f"v_dz{s}"], vd_b[s] + i + 1)
                            t.wait_ge(sm[f"kc{s}{ss % 2}"], kcs_cnt[(s, ss)])
                            t.wait_ge(sm[f"v_w{s}"], vw_b[s] + i)
                            c.inc(t.matmul(p_wb[s][:, :], kcs_sb[s][ss % 2][:, cs * 128:(cs + 1) * 128],
                                           dz_sb[s], start=True, stop=True), f"t_g{s}", 1)

                @block.vector
                def _(v, half=half, tz_b=tz_b, tg_b=tg_b, vd_b=vd_b, vw_b=vw_b, sw_b=sw_b, vcs_cnt=vcs_cnt):
                    if half == 0:
                        for nm in ("d_kt0", "d_kt1", "d_vt0", "d_vt1"):
                            v.wait_ge(sm[nm], ct[nm])
                    for s in range(2):
                        v.tensor_copy(wf[s][0], ident_f[:])
                    for i in range(KSCAN):
                        ss, cs = i // 16, i % 16
                        for s in range(2):
                            v.wait_ge(sm[f"t_z{s}"], tz_b[s] + i + 1)
                            v.wait_ge(sm[f"vc{s}{ss % 2}"], vcs_cnt[(s, ss)])
                            v.wait_ge(sm[f"t_g{s}"], tg_b[s] + i)
                            c.inc(v.scalar_tensor_tensor(dz_sb[s], zb[s][0:16, 0:128], ninv16,
                                                         vcs_sb[s][ss % 2][:, cs * 128:(cs + 1) * 128],
                                                         ALU.mult, ALU.add), f"v_dz{s}", 1)
                        for s in range(2):
                            v.wait_ge(sm[f"t_g{s}"], tg_b[s] + i + 1)
                            v.wait_ge(sm[f"s_w{s}"], sw_b[s] + i)
                            c.inc(v.tensor_add(wf[s][(i + 1) % 2], wf[s][i % 2], p_wb[s][:, :]), f"v_w{s}", 1)

                @block.sync
                def _(sy, half=half, zc_b=zc_b):
                    for ss in range(KSCAN // 16 if KZ == 1 else 0):
                        for s in range(2):
                            hl = half * 2 + s
                            sy.wait_ge(sm[f"s_zc{s}"], zc_b[s] + (ss + 1) * 16)
                            c.dma(sy, f"zw{s}{ss % 2}",
                                  z_l.ap()[hl, :, ss * 16:(ss + 1) * 16, :].rearrange("p c e -> p (c e)"),
                                  z_sub[s][ss % 2])

                @block.scalar
                def _(s_, half=half, tz_b=tz_b, tg_b=tg_b, zc_b=zc_b, sw_b=sw_b,
                      vw_b=vw_b, vd_b=vd_b, zwb_sl=zwb_sl):
                    if half == 0:
                        for nm in ("d_kt0", "d_kt1", "d_vt0", "d_vt1"):
                            s_.wait_ge(sm[nm], ct[nm])
                    for s in range(2):
                        if half == 1:
                            s_.wait_ge(sm[f"t_z{s}"], tz_b[s])
                        c.inc(s_.copy(wsb[s], ident_b[:]), f"s_w{s}", 1)
                    for i in range(KSCAN):
                        ss, cs = i // 16, i % 16
                        for s in range(2):
                            hl = half * 2 + s
                            if KZ:
                                s_.wait_ge(sm[f"t_z{s}"], tz_b[s] + i + 1)
                                s_.wait_ge(sm[f"v_dz{s}"], vd_b[s] + i + 1)
                                if cs == 0 and ss >= 2 and KZ == 1:
                                    s_.wait_ge(sm[f"zw{s}{ss % 2}"], zwb_sl[(s, ss % 2)] + 16 * (ss // 2))
                                c.inc(s_.copy(z_sub[s][ss % 2][:, cs * 128:(cs + 1) * 128],
                                              zb[s][0:16, 0:128]), f"s_zc{s}", 1)
                            s_.wait_ge(sm[f"v_w{s}"], vw_b[s] + i + 1)
                            c.inc(s_.copy(wsb[s], wf[s][(i + 1) % 2]), f"s_w{s}", 1)

        if max_phase >= 5:
            ar.reset()
            wgu = [ar.take(KC * 2 * 128, BF16) for _ in range(2)]
            wdn = [ar.take(MC * 128, BF16) for _ in range(2)]
            t_e = ar.take(MC * TOK, BF16)
            silu_g = [ar.take(512, F32) for _ in range(2)]
            su = ar.take(512, F32)
            devac_w = [ar.take(TOK, F32) for _ in range(2)]
            m5 = ar.mark()
            comb_row = ar.take(TOK, F32)[0:1, :]
            comb_sb = ar.take(TOK, F32)
            pga = [p_a, p_c]
            pgu = [p_b, p_f]
            SCAN_DONE = {k: ct[k] for k in ("v_w0", "v_w1", "v_dz0", "v_dz1", "s_zc0", "s_zc1", "s_w0", "s_w1")}

            @block.gpsimd
            def _(g):
                g.wait_ge(sm["v_cf"], NT)
                g.wait_ge(sm["s_w0"], SCAN_DONE["s_w0"])
                g.wait_ge(sm["s_w1"], SCAN_DONE["s_w1"])
                for s in range(2):
                    for j in range(2):
                        g.wait_ge(sm[f"zw{s}{j}"], ct[f"zw{s}{j}"])
                for e in range(E):
                    g.wait_ge(sm["t_bc"], 4 * e)
                    c.dma(g, "d_cr", comb_row, comb_fm[e:e + 1, :])
                    for m in range(MC):
                        gi = e * MC + m
                        if gi >= 2:
                            g.wait_ge(sm["t_mu"], (gi - 1) * 4)
                        vw = wgu[gi % 2].rearrange("p (k g m) -> p k g m", k=KC, g=2)
                        c.dma(g, f"d_gu{gi % 2}", vw[:, :, 0, :],
                              wg_in.ap()[e, :, m * 128:(m + 1) * 128].rearrange("(k p) m -> p k m", p=128))
                        c.dma(g, f"d_gu{gi % 2}", vw[:, :, 1, :],
                              wu_in.ap()[e, :, m * 128:(m + 1) * 128].rearrange("(k p) m -> p k m", p=128))
                        if gi == 1 and max_phase >= 4:
                            c.inc(g.collective_compute(
                                "AllGather", ALU.bypass, replica_groups=[[0, 1, 2, 3], [4, 5, 6, 7]],
                                ins=[z_l.ap().rearrange("l p c e -> (l p) (c e)")],
                                outs=[z_gt.ap().rearrange("r l p c e -> (r l p) (c e)")]), "cc", 1)
                    for o in range(E + 1):
                        if o < E:
                            oi = e * E + o
                            if oi >= 2:
                                g.wait_ge(sm["t_dn"], (oi - 1) * 4)
                            c.dma(g, f"d_dn{oi % 2}", wdn[oi % 2].rearrange("p (k m) -> p k m", k=MC),
                                  wd_in.ap()[e, :, o * 128:(o + 1) * 128].rearrange("(k p) m -> p k m", p=128))
                        if o >= 1:
                            oi = e * E + (o - 1)
                            g.wait_ge(sm["s_ev"], 4 * (oi + 1))
                            c.dma(g, f"d_ac{oi % 2}",
                                  routed_d.ap()[(o - 1) * 128:o * 128, :],
                                  devac_w[oi % 2],
                                  accum_op=(ALU.add if e > 0 else ALU.bypass))

            @block.tensor
            def _(t):
                for k in ("v_w0", "v_w1", "v_dz0", "v_dz1", "s_zc0", "s_zc1"):
                    t.wait_ge(sm[k], SCAN_DONE[k])
                for e in range(E):
                    t.wait_ge(sm["d_cr"], 16 * (e + 1))
                    for tg in range(NTG):
                        if e > 0:
                            t.wait_ge(sm["s_ev"], e * 64)
                        t.wait_ge(sm["v_cs"], 4 * e + tg)
                        c.inc(t.matmul(p_e[:, :], ones_r[:], comb_row[:, tg * 512:(tg + 1) * 512],
                                       start=True, stop=True), "t_bc", 1)
                    for m in range(MC):
                        gi = e * MC + m
                        t.wait_ge(sm[f"d_gu{gi % 2}"], 32 * (gi // 2 + 1))
                        for tg in range(NTG):
                            gseq = gi * 4 + tg
                            if gseq >= 2:
                                t.wait_ge(sm["s_si"], gseq - 1)
                            for k in range(KC):
                                ins = t.matmul(pga[gseq % 2][:, :], wgu[gi % 2][:, (k * 2) * 128:(k * 2 + 1) * 128],
                                               xn[:, k * TOK + tg * 512: k * TOK + (tg + 1) * 512],
                                               start=(k == 0), stop=(k == KC - 1))
                            c.inc(ins, "t_mg", 1)
                            if gseq >= 2:
                                t.wait_ge(sm["v_su"], gseq - 1)
                            for k in range(KC):
                                ins = t.matmul(pgu[gseq % 2][:, :], wgu[gi % 2][:, (k * 2 + 1) * 128:(k * 2 + 2) * 128],
                                               xn[:, k * TOK + tg * 512: k * TOK + (tg + 1) * 512],
                                               start=(k == 0), stop=(k == KC - 1))
                            c.inc(ins, "t_mu", 1)
                    for o in range(E):
                        oi = e * E + o
                        t.wait_ge(sm[f"d_dn{oi % 2}"], 16 * (oi // 2 + 1))
                        for tg in range(NTG):
                            di = oi * NTG + tg
                            if o == 0 and tg == 0:
                                t.wait_ge(sm["v_te"], (e + 1) * 44)
                                t.wait_ge(sm["v_cs"], 4 * (e + 1))
                            t.wait_ge(sm["s_ev"], di)
                            for k in range(MC):
                                ins = t.matmul(p_e[:, :], wdn[oi % 2][:, k * 128:(k + 1) * 128],
                                               t_e[:, k * TOK + tg * 512: k * TOK + (tg + 1) * 512],
                                               start=(k == 0), stop=(k == MC - 1))
                            c.inc(ins, "t_dn", 1)

            @block.scalar
            def _(s):
                for e in range(E):
                    for m in range(MC):
                        for tg in range(NTG):
                            gseq = (e * MC + m) * 4 + tg
                            s.wait_ge(sm["t_mg"], gseq + 1)
                            if gseq >= 2:
                                s.wait_ge(sm["v_su"], gseq - 1)
                            c.inc(s.activation(silu_g[gseq % 2], pga[gseq % 2][:, :], AF.Silu), "s_si", 1)
                    for o in range(E):
                        for tg in range(NTG):
                            di = (e * E + o) * NTG + tg
                            oi = e * E + o
                            s.wait_ge(sm["t_dn"], di + 1)
                            if tg == 0 and oi >= 2:
                                s.wait_ge(sm[f"d_ac{oi % 2}"], 16 * (oi // 2))
                            c.inc(s.copy(devac_w[oi % 2][:, tg * 512:(tg + 1) * 512], p_e[:, :]), "s_ev", 1)

            @block.vector
            def _(v):
                for e in range(E):
                    for tg in range(NTG):
                        v.wait_ge(sm["t_bc"], 4 * e + tg + 1)
                        c.inc(v.tensor_copy(comb_sb[:, tg * 512:(tg + 1) * 512], p_e[:, :]), "v_cs", 1)
                    for m in range(MC):
                        for tg in range(NTG):
                            gseq = (e * MC + m) * 4 + tg
                            v.wait_ge(sm["s_si"], gseq + 1)
                            v.wait_ge(sm["t_mu"], gseq + 1)
                            c.inc(v.tensor_mul(su, silu_g[gseq % 2], pgu[gseq % 2][:, :]), "v_su", 1)
                            if m == 0 and tg == 0 and e > 0:
                                v.wait_ge(sm["t_dn"], e * 64)
                            c.inc(v.tensor_mul(t_e[:, m * TOK + tg * 512:m * TOK + (tg + 1) * 512],
                                               su, comb_sb[:, tg * 512:(tg + 1) * 512]), "v_te", 1)

        if max_phase >= 5:
            ar.rollback(m5)
            sg_row = ar.take(TOK, F32)[0:1, :]
            sg_bc = ar.take(TOK, F32)
            shd_big = [ar.take(SHC * 128, BF16) for _ in range(2)]
            devac = [ar.take(512, F32) for _ in range(2)]
            t_sh = t_e
            gu_b = {0: ct["d_gu0"], 1: ct["d_gu1"]}
            dn_b = {0: ct["d_dn0"], 1: ct["d_dn1"]}
            ac_b = {0: ct["d_ac0"], 1: ct["d_ac1"]}
            tdn_b = ct["t_dn"]
            tmg_b = ct["t_mg"]
            tmu_b = ct["t_mu"]
            ssi_b = ct["s_si"]
            vte_b = ct["v_te"]
            VCS_ALL = ct["v_cs"]
            SEV_ALL = ct["s_ev"]
            SSI_ALL = ct["s_si"]
            VSU_ALL = ct["v_su"]

            @block.tensor
            def _(t):
                t.wait_ge(sm["d_w"], 64)
                t.wait_ge(sm["s_ev"], SEV_ALL)
                t.wait_ge(sm["v_cs"], VCS_ALL)
                t.wait_ge(sm["s_si"], SSI_ALL)
                t.wait_ge(sm["v_su"], VSU_ALL)
                for tg in range(NTG):
                    t.wait_ge(sm["v_sg"], 2 * tg)
                    for k in range(KC):
                        ins = t.matmul(p_e[0:1, :], sgw_sb[:, k:k + 1],
                                       xn[:, k * TOK + tg * 512: k * TOK + (tg + 1) * 512],
                                       start=(k == 0), stop=(k == KC - 1))
                    c.inc(ins, "t_sg", 1)
                    t.wait_ge(sm["s_sg"], tg + 1)
                    c.inc(t.matmul(p_e[:, :], ones_r[:], sg_row[:, tg * 512:(tg + 1) * 512],
                                   start=True, stop=True), "t_sg", 1)
                for tg in range(NTG):
                    for ic in range(SHC):
                        gi = tg * SHC + ic
                        t.wait_ge(sm[f"d_gu{gi % 2}"], gu_b[gi % 2] + 32 * (gi // 2 + 1))
                        t.wait_ge(sm["s_si"], ssi_b + max(0, gi - 1))
                        for k in range(KC):
                            ins = t.matmul(pga[gi % 2][:, :], wgu[gi % 2][:, (k * 2) * 128:(k * 2 + 1) * 128],
                                           xn[:, k * TOK + tg * 512: k * TOK + (tg + 1) * 512],
                                           start=(k == 0), stop=(k == KC - 1))
                        c.inc(ins, "t_mg", 1)
                        t.wait_ge(sm["v_te"], vte_b + max(0, gi - 1))
                        for k in range(KC):
                            ins = t.matmul(pgu[gi % 2][:, :], wgu[gi % 2][:, (k * 2 + 1) * 128:(k * 2 + 2) * 128],
                                           xn[:, k * TOK + tg * 512: k * TOK + (tg + 1) * 512],
                                           start=(k == 0), stop=(k == KC - 1))
                        c.inc(ins, "t_mu", 1)
                    for o in range(E):
                        oi = tg * E + o
                        t.wait_ge(sm[f"d_dn{oi % 2}"], dn_b[oi % 2] + 16 * (oi // 2 + 1))
                        if oi == 0:
                            t.wait_ge(sm["v_sg"], 8)
                        if o == 0:
                            t.wait_ge(sm["v_te"], vte_b + (tg + 1) * SHC)
                        t.wait_ge(sm["v_sh"], oi)
                        for k in range(SHC):
                            ins = t.matmul(p_e[:, :], shd_big[oi % 2][:, k * 128:(k + 1) * 128],
                                           t_sh[:, k * 512:(k + 1) * 512],
                                           start=(k == 0), stop=(k == SHC - 1))
                        c.inc(ins, "t_dn", 1)

            @block.scalar
            def _(s):
                for tg in range(NTG):
                    s.wait_ge(sm["t_sg"], 2 * tg + 1)
                    c.inc(s.activation(sg_row[:, tg * 512:(tg + 1) * 512], p_e[0:1, :], AF.Sigmoid), "s_sg", 1)
                for tg in range(NTG):
                    for ic in range(SHC):
                        gi = tg * SHC + ic
                        s.wait_ge(sm["t_mg"], tmg_b + gi + 1)
                        if gi >= 2:
                            s.wait_ge(sm["v_te"], vte_b + gi - 1)
                        c.inc(s.activation(silu_g[gi % 2], pga[gi % 2][:, :], AF.Silu), "s_si", 1)

            @block.vector
            def _(v):
                for tg in range(NTG):
                    v.wait_ge(sm["t_sg"], 2 * tg + 2)
                    c.inc(v.tensor_copy(sg_bc[:, tg * 512:(tg + 1) * 512], p_e[:, :]), "v_sg", 2)
                for tg in range(NTG):
                    for ic in range(SHC):
                        gi = tg * SHC + ic
                        v.wait_ge(sm["s_si"], ssi_b + gi + 1)
                        v.wait_ge(sm["t_mu"], tmu_b + gi + 1)
                        if ic == 0 and tg > 0:
                            v.wait_ge(sm["t_dn"], tdn_b + tg * E)
                        c.inc(v.tensor_mul(t_sh[:, ic * 512:(ic + 1) * 512], silu_g[gi % 2],
                                           pgu[gi % 2][:, :]), "v_te", 1)
                    for o in range(E):
                        oi = tg * E + o
                        v.wait_ge(sm["t_dn"], tdn_b + oi + 1)
                        if oi >= 2:
                            v.wait_ge(sm[f"d_ac{oi % 2}"], ac_b[oi % 2] + 16 * (oi // 2))
                        c.inc(v.tensor_mul(devac[oi % 2], p_e[:, :],
                                           sg_bc[:, tg * 512:(tg + 1) * 512]), "v_sh", 1)

            @block.gpsimd
            def _(g):
                for tg in range(NTG):
                    for ic in range(SHC):
                        gi = tg * SHC + ic
                        g.wait_ge(sm["t_mu"], tmu_b + max(0, gi - 1))
                        vw = wgu[gi % 2].rearrange("p (k g m) -> p k g m", k=KC, g=2)
                        c.dma(g, f"d_gu{gi % 2}", vw[:, :, 0, :],
                              shg_in.ap()[:, ic * 128:(ic + 1) * 128].rearrange("(k p) m -> p k m", p=128))
                        c.dma(g, f"d_gu{gi % 2}", vw[:, :, 1, :],
                              shu_in.ap()[:, ic * 128:(ic + 1) * 128].rearrange("(k p) m -> p k m", p=128))
                    for o in range(E):
                        oi = tg * E + o
                        if oi >= 2:
                            g.wait_ge(sm["t_dn"], tdn_b + oi - 1)
                        c.dma(g, f"d_dn{oi % 2}", shd_big[oi % 2].rearrange("p (k m) -> p k m", k=SHC),
                              shd_in.ap()[:, o * 128:(o + 1) * 128].rearrange("(k p) m -> p k m", p=128))
                        g.wait_ge(sm["v_sh"], oi + 1)
                        c.dma(g, f"d_ac{oi % 2}",
                              routed_d.ap()[o * 128:(o + 1) * 128, tg * 512:(tg + 1) * 512],
                              devac[oi % 2], accum_op=ALU.add)

        if max_phase >= 6:
            ar.reset()
            gam_b = ar.take(D, F32)
            bet_b = ar.take(D, F32)
            z_tile = ar.take(D, BF16)
            lnsc = ar.take(D, F32)
            lnz_f = ar.take(D, F32)
            lnfin = ar.take(D, F32)
            lnz_fm = ar.take(4 * D, BF16)
            wo_sb = [ar.take(KC * 128, BF16) for _ in range(2)]
            rsl = [ar.take(512, F32) for _ in range(2)]
            asm1 = ar.take(512, F32)
            out_tm = [ar.take(D, F32) for _ in range(4)]
            p_zf = [p_w1[:, :], p_c[:, 0:128]]
            pab = [p_a, p_b]

            b6 = {k: ct[k] for k in ("s_sq", "v_st", "s_sd", "v_nm", "s_xn",
                                     "t_tr", "v_xn", "d_zl", "v_g")}
            AC0, AC1 = ct["d_ac0"], ct["d_ac1"]

            @block.sync
            def _(sy):
                sy.wait_ge(sm["d_ac0"], AC0)
                sy.wait_ge(sm["d_ac1"], AC1)
                c.dma(sy, "d_gb", gam_b, gam_in.ap())
                c.dma(sy, "d_gb", bet_b, bet_in.ap())
                for tg in range(NTG):
                    for o in range(E):
                        woi = tg * E + o
                        if woi >= 2:
                            sy.wait_ge(sm["v_as"], woi - 1)
                        c.dma(sy, f"d_rs{woi % 2}", rsl[woi % 2],
                              routed_d.ap()[o * 128:(o + 1) * 128, tg * 512:(tg + 1) * 512])


            @block.gpsimd
            def _(g):
                g.wait_ge(sm["cc"], ct["cc"])
                g.wait_ge(sm["d_ac0"], AC0)
                g.wait_ge(sm["d_ac1"], AC1)
                for T in range(NT):
                    c.dma(g, "d_out", out_t.ap()[T * 128:(T + 1) * 128, :],
                          x_in.ap()[T * 128:(T + 1) * 128, :])
                for tg in range(NTG):
                    for jj in range(4):
                        T = tg * 4 + jj
                        if T > 0:
                            g.wait_ge(sm["s_xn"], b6["s_xn"] + T)
                        g.wait_ge(sm["d_zl"], b6["d_zl"] + T * 256)

                        def bodyz(lc, T=T):
                            b_, q_ = lc // 4, lc % 4
                            c0 = q_ * CPR + T * 8
                            for rh in range(4):
                                for hl in range(HL):
                                    c.inc(g.dma_start(
                                        out=z_tile[:, (rh * 4 + hl) * 128:(rh * 4 + hl + 1) * 128],
                                        in_=z_gt.ap()[rh, hl, :, c0:c0 + 8, :]
                                        .rearrange("p c e -> c p e")), "d_zl", 16)
                        c.core_switch(g, bodyz)
                    for o in range(E):
                        woi = tg * E + o
                        if woi >= 2:
                            g.wait_ge(sm["t_wo"], woi - 1)
                        c.dma(g, f"d_wo{woi % 2}", wo_sb[woi % 2].rearrange("p (k m) -> p k m", k=KC),
                              wo_t.ap()[:, o * 128:(o + 1) * 128].rearrange("(k p) m -> p k m", p=128))
                    for jj in range(4):
                        T = tg * 4 + jj
                        g.wait_ge(sm["v_ot"], 64 * (tg + 1))
                        c.dma(g, "d_out", out_t.ap()[T * 128:(T + 1) * 128, :],
                              out_tm[jj], accum_op=ALU.add)

            @block.scalar
            def _(s):
                for T in range(NT):
                    s.wait_ge(sm["d_zl"], b6["d_zl"] + (T + 1) * 256)
                    c.inc(s.square(lnsc, z_tile), "s_sq", 1)
                    s.wait_ge(sm["v_st"], b6["v_st"] + T + 1)
                    c.inc(s.activation(stat[:, 4:5], stat[:, 6:7], AF.Sqrt,
                                       bias=consts[:, C_EPS:C_EPS + 1]), "s_sd", 1)
                    s.wait_ge(sm["v_nm"], b6["v_nm"] + T + 1)
                    c.inc(s.activation(lnsc, z_tile, AF.Identity,
                                       bias=stat[:, 6:7], scale=stat[:, 5:6]), "s_xn", 1)

            @block.vector
            def _(v):
                v.wait_ge(sm["d_gb"], 32)
                for tg in range(NTG):
                    for jj in range(4):
                        T = tg * 4 + jj
                        v.wait_ge(sm["d_zl"], b6["d_zl"] + (T + 1) * 256)
                        def G(inst):
                            gvv = c.inc(inst, "v_g", 1)
                            v.wait_ge(sm["v_g"], gvv)
                        G(v.reduce_sum(stat[:, 0:1], z_tile, axis=AX))
                        v.wait_ge(sm["s_sq"], b6["s_sq"] + T + 1)
                        G(v.reduce_sum(stat[:, 1:2], lnsc, axis=AX))
                        G(v.tensor_scalar(stat[:, 2:3], stat[:, 0:1], consts[:, C_INVD:C_INVD + 1], None, ALU.mult))
                        G(v.tensor_scalar(stat[:, 3:4], stat[:, 1:2], consts[:, C_INVD:C_INVD + 1], None, ALU.mult))
                        G(v.tensor_mul(stat[:, 7:8], stat[:, 2:3], stat[:, 2:3]))
                        c.inc(v.tensor_sub(stat[:, 6:7], stat[:, 3:4], stat[:, 7:8]), "v_st", 1)
                        v.wait_ge(sm["s_sd"], b6["s_sd"] + T + 1)
                        G(v.reciprocal(stat[:, 5:6], stat[:, 4:5]))
                        c.inc(v.tensor_scalar(stat[:, 6:7], stat[:, 2:3], stat[:, 5:6],
                                              consts[:, C_NEG1:C_NEG1 + 1], ALU.mult, ALU.mult), "v_nm", 1)
                        v.wait_ge(sm["s_xn"], b6["s_xn"] + T + 1)
                        if T >= 1:
                            v.wait_ge(sm["t_tr"], b6["t_tr"] + 16 * T)
                        v.tensor_mul(lnz_f, lnsc, gam_b)
                        c.inc(v.tensor_add(lnfin, lnz_f, bet_b), "v_g", 1)
                        for k in range(KC):
                            v.wait_ge(sm["t_tr"], b6["t_tr"] + 16 * T + k + 1)
                            dst = lnz_fm[:, k * 512 + jj * 128: k * 512 + (jj + 1) * 128]
                            c.inc(v.tensor_copy(dst, trb[k % 2][:, :]), "v_xn", 1)
                    for o in range(E):
                        woi = tg * E + o
                        if o == 0 and tg > 0:
                            v.wait_ge(sm["d_out"], 16 * NT + 64 * tg)
                        v.wait_ge(sm["t_wo"], woi + 1)
                        v.wait_ge(sm[f"d_rs{woi % 2}"], 16 * (woi // 2 + 1))
                        c.inc(v.tensor_add(asm1, pab[woi % 2][:, :], rsl[woi % 2]), "v_as", 1)
                        for jj in range(4):
                            v.wait_ge(sm["t_ft"], 4 * woi + jj + 1)
                            c.inc(v.tensor_copy(out_tm[jj][:, o * 128:(o + 1) * 128],
                                                p_zf[jj % 2]), "v_ot", 1)
                    for jj in range(4):
                        c.inc(v.engine_nop(), "v_ox", 1)

            @block.tensor
            def _(t):
                t.wait_ge(sm["s_ev"], ct["s_ev"])
                t.wait_ge(sm["v_sh"], ct["v_sh"])
                for tg in range(NTG):
                    for jj in range(4):
                        T = tg * 4 + jj
                        t.wait_ge(sm["v_g"], b6["v_g"] + 7 * T + 7)
                        for k in range(KC):
                            idx = 16 * T + k
                            if idx >= 1:
                                t.wait_ge(sm["v_xn"], b6["v_xn"] + idx - 1)
                            c.inc(t.transpose(trb[idx % 2][:, :],
                                              lnfin[:, k * 128:(k + 1) * 128], ident_f[:]), "t_tr", 1)
                    for o in range(E):
                        woi = tg * E + o
                        t.wait_ge(sm[f"d_wo{woi % 2}"], 16 * (woi // 2 + 1))
                        t.wait_ge(sm["v_as"], max(0, woi - 1))
                        if o == 0:
                            t.wait_ge(sm["v_xn"], b6["v_xn"] + 64 * (tg + 1))
                        for k in range(KC):
                            ins = t.matmul(pab[woi % 2][:, :], wo_sb[woi % 2][:, k * 128:(k + 1) * 128],
                                           lnz_fm[:, k * 512:(k + 1) * 512],
                                           start=(k == 0), stop=(k == KC - 1))
                        c.inc(ins, "t_wo", 1)
                        t.wait_ge(sm["v_as"], woi + 1)
                        for jj in range(4):
                            t.wait_ge(sm["v_ot"], max(0, 4 * woi + jj - 1))
                            c.inc(t.transpose(p_zf[jj % 2], asm1[:, jj * 128:(jj + 1) * 128],
                                              ident_f[:]), "t_ft", 1)

        if max_phase < 6:
            @block.sync
            def _(sy):
                sy.wait_ge(sm["v_xn"], ct["v_xn"])
                for T in range(NT):
                    c.dma(sy, "d_out", out_t.ap()[T * 128:(T + 1) * 128, :],
                          xn[:, 0:4096].bitcast(F32))

        if debug:
            @block.gpsimd
            def _(g):
                g.wait_ge(sm["v_xn"], ct["v_xn"])
                c.dma(g, "d_dbg", dbg["xn"].ap(), xn[:])
                g.wait_ge(sm["v_cf"], ct["v_cf"])
                c.dma(g, "d_dbg", dbg["comb"].ap(), comb_fm[:])
                if max_phase >= 1:
                    for nm in ("d_kr0", "d_kr1", "d_kt0", "d_kt1", "d_vt0", "d_vt1"):
                        g.wait_ge(sm[nm], ct[nm])
                    c.dma(g, "d_dbg", dbg["kfm"].ap(), kfm_l.ap())
                    c.dma(g, "d_dbg", dbg["ktm"].ap(), ktm_l.ap())
                    c.dma(g, "d_dbg", dbg["vtm"].ap(), vtm_l.ap())
                if max_phase >= 3:
                    for s in range(2):
                        for j in range(2):
                            g.wait_ge(sm[f"zw{s}{j}"], ct[f"zw{s}{j}"])
                    c.dma(g, "d_dbg", dbg["zl"].ap(), z_l.ap())
                if max_phase >= 5:
                    g.wait_ge(sm["d_ac0"], ct["d_ac0"])
                    g.wait_ge(sm["d_ac1"], ct["d_ac1"])
                    c.dma(g, "d_dbg", dbg["routed"].ap(), routed_d.ap())

        @block.sync
        def _(sy):
            sy.wait_ge(sm["d_out"], ct["d_out"])
            sy.wait_ge(sm["d_dbg"], ct["d_dbg"])

    return nc


def make_idx_table(core):
    b, q = core // 4, core % 4
    t = np.zeros((128, N_IDX), np.int32)
    for r in range(4):
        for hl in range(HL):
            h = 4 * q + hl
            t[:, IXC_KFM + r * 4 + hl] = ((4 * b + r) * H + h) * 128 + np.arange(128)
            t[0:16, IXC_KTM + r * 4 + hl] = ((4 * b + r) * H + h) * 16 + np.arange(16)
    for T in range(NT):
        for rh in range(4):
            for hl in range(HL):
                col = IXC_Z + T * 16 + rh * 4 + hl
                cc = np.arange(128) // 16
                pp = np.arange(128) % 16
                cglob = q * CPR + T * 8 + cc
                t[:, col] = (((4 * b + rh) * HL + hl) * 16 + pp) * NCHUNK + cglob
    return t


def host_inputs(inputs):
    x = np.asarray(inputs["L0_attn_x"], np.float32)
    Wk = np.asarray(inputs["L0_attn_Wk"], np.float32)
    Wv = np.asarray(inputs["L0_attn_Wv"], np.float32)
    Wo = np.asarray(inputs["L0_attn_Wo"], np.float32)
    gam = np.asarray(inputs["L0_attn_ln_gamma"], np.float32)
    bet = np.asarray(inputs["L0_attn_ln_beta"], np.float32)
    rw = np.asarray(inputs["L0_ffn_router_w"], np.float32)

    shared = {
        "wk_t": np.ascontiguousarray(Wk.T), "wv_t": np.ascontiguousarray(Wv.T),
        "wo_t": np.ascontiguousarray(Wo.T), "rw_t": np.ascontiguousarray(rw.T),
        "wg_in": np.ascontiguousarray(np.asarray(inputs["L0_ffn_w_gate"], np.float32)),
        "wu_in": np.ascontiguousarray(np.asarray(inputs["L0_ffn_w_up"], np.float32)),
        "wd_in": np.ascontiguousarray(np.asarray(inputs["L0_ffn_w_down"], np.float32)),
        "shg_in": np.ascontiguousarray(np.asarray(inputs["L0_ffn_sh_gate"], np.float32)),
        "shu_in": np.ascontiguousarray(np.asarray(inputs["L0_ffn_sh_up"], np.float32)),
        "shd_in": np.ascontiguousarray(np.asarray(inputs["L0_ffn_sh_down"], np.float32)),
        "sgw_in": np.ascontiguousarray(
            np.asarray(inputs["L0_ffn_sh_gate_w"], np.float32).reshape(KC, 128).T),
        "gam_in": np.ascontiguousarray(np.broadcast_to(gam[None, :], (128, D))),
        "bet_in": np.ascontiguousarray(np.broadcast_to(bet[None, :], (128, D))),
        "ident_in": np.eye(128, dtype=np.float32),
        "ones_in": np.ones((1, 128), np.float32),
    }
    P = np.zeros((128, 128), np.float32)
    P[np.arange(64), np.arange(64) + 64] = -1.0
    P[np.arange(64) + 64, np.arange(64)] = 1.0
    shared["ropeP_in"] = np.ascontiguousarray(P.T)
    cvals = np.zeros(8, np.float32)
    cvals[C_INVD] = 1.0 / D
    cvals[C_EPS] = 1e-5
    cvals[C_NEG1] = -1.0
    cvals[C_TINY] = 1e-9
    cvals[C_NBIG] = -1e9
    cvals[C_ONE] = 1.0
    cvals[C_NINV] = -INV_N
    cvals[C_INVN] = INV_N
    shared["consts_in"] = np.ascontiguousarray(np.broadcast_to(cvals[None, :], (128, 8)))

    inv_freq = 1.0 / (10000.0 ** (np.arange(0, DH, 2, dtype=np.float64) / DH))
    in_maps = []
    for core in range(8):
        b, q = core // 4, core % 4
        pos = q * TOK + np.arange(TOK, dtype=np.float64)
        fr = np.outer(pos, inv_freq)
        emb = np.concatenate([fr, fr], axis=1)
        m = dict(shared)
        m["x_in"] = np.ascontiguousarray(x[b, q * TOK:(q + 1) * TOK])
        m["cos_in"] = np.ascontiguousarray(np.cos(emb).T.astype(np.float32))
        m["sin_in"] = np.ascontiguousarray(np.sin(emb).T.astype(np.float32))
        in_maps.append(m)
    return in_maps


def kernel(**inputs):
    in_maps = host_inputs(inputs)
    nc = build_nc(max_phase=6, debug=False)
    res = run_bass_kernel_spmd(nc, in_maps, core_ids=list(range(8)))
    out = np.empty((B, S, D), np.float32)
    for core in range(8):
        b, q = core // 4, core % 4
        out[b, q * TOK:(q + 1) * TOK] = res.results[core]["out"]
    return out


# version 08 (final)
# speedup vs baseline: 1.0624x; 1.0029x over previous
import os
import numpy as np
import ml_dtypes
from contextlib import ExitStack
KSUB = int(os.environ.get("KSUB", "9"))
KSCAN = int(os.environ.get("KSCAN", "512"))
KZ = int(os.environ.get("KZ", "1"))

import concourse.bass as bass
import concourse.mybir as mybir
from concourse.bass_utils import run_bass_kernel_spmd

F32 = mybir.dt.float32
BF16 = mybir.dt.bfloat16
U8 = mybir.dt.uint8
I32 = mybir.dt.int32
AF = mybir.ActivationFunctionType
ALU = mybir.AluOpType
AX = mybir.AxisListType.X
BF = ml_dtypes.bfloat16

B, S, D = 2, 8192, 2048
H, DH, MBL = 16, 128, 16
E, I_MOE, I_SH = 16, 1408, 5632
TOK = 2048
NT = TOK // 128
KC = D // 128
NTG = TOK // 512
MC = I_MOE // 128
SHC = I_SH // 128
NCHUNK = S // MBL
CPR = TOK // MBL
INV_N = 2.0 / (MBL * DH)
HL = 4
NSUB = NCHUNK // 16

C_INVD, C_EPS, C_NEG1, C_TINY, C_NBIG, C_ONE, C_NINV, C_INVN = range(8)

IXC_KFM = 0
IXC_KTM = 16
IXC_Z = 32
N_IDX = 32 + 256


class Ctx:
    def __init__(self, nc, es):
        self.nc, self.es = nc, es
        self.sem, self.cnt = {}, {}

    def sem_new(self, *names):
        for n in names:
            self.sem[n] = self.es.enter_context(self.nc.semaphore(n))
            self.cnt[n] = 0

    def inc(self, inst, name, n=1):
        inst.then_inc(self.sem[name], n)
        self.cnt[name] += n
        return self.cnt[name]

    def core_switch(self, eng, body_fn):
        snap = dict(self.cnt)
        final = None
        for arm in self.nc.switch_core_id(engines=[eng], n=128):
            for k in self.cnt:
                self.cnt[k] = snap[k]
            body_fn(arm.logical % 8)
            final = dict(self.cnt)
        self.cnt.update(final)

    def dma(self, eng, sem_name, out, in_, pre_wait=True, **kw):
        if pre_wait:
            eng.wait_ge(self.sem[sem_name], self.cnt[sem_name])
        return self.inc(eng.dma_start(out=out, in_=in_, **kw), sem_name, 16)


class Arena:
    def __init__(self, t, nbytes):
        self.t, self.nbytes, self.off = t, nbytes, 0

    def reset(self):
        self.off = 0
        return self

    def mark(self):
        return self.off

    def rollback(self, m):
        self.off = m

    def take(self, cols, dtype):
        sz = cols * mybir.dt.size(dtype)
        sz_al = (sz + 31) & ~31
        assert self.off + sz_al <= self.nbytes, f"arena overflow {self.off}+{sz_al}"
        v = self.t[:, self.off:self.off + sz].bitcast(dtype)
        self.off += sz_al
        return v


def build_nc(max_phase=6, debug=False):
    nc = bass.Bass("TRN2", target_bir_lowering=False)
    es = ExitStack()
    c = Ctx(nc, es)
    sm, ct = c.sem, c.cnt

    dT = lambda n, sh, dt=F32, kind="ExternalInput": nc.dram_tensor(n, sh, dt, kind=kind)
    x_in = dT("x_in", [TOK, D])
    wk_t = dT("wk_t", [D, D])
    wv_t = dT("wv_t", [D, D])
    wo_t = dT("wo_t", [D, D])
    rw_t = dT("rw_t", [D, E])
    wg_in = dT("wg_in", [E, D, I_MOE])
    wu_in = dT("wu_in", [E, D, I_MOE])
    wd_in = dT("wd_in", [E, I_MOE, D])
    shg_in = dT("shg_in", [D, I_SH])
    shu_in = dT("shu_in", [D, I_SH])
    shd_in = dT("shd_in", [I_SH, D])
    sgw_in = dT("sgw_in", [128, KC])
    gam_in = dT("gam_in", [128, D])
    bet_in = dT("bet_in", [128, D])
    cos_in = dT("cos_in", [128, TOK])
    sin_in = dT("sin_in", [128, TOK])
    ident_in = dT("ident_in", [128, 128])
    ropeP_in = dT("ropeP_in", [128, 128])
    ones_in = dT("ones_in", [1, 128])
    consts_in = dT("consts_in", [128, 8])
    out_t = dT("out", [TOK, D], F32, kind="ExternalOutput")

    kfm_l = nc.dram_tensor("kfm_l", [H, 128, TOK], BF16)
    ktm_l = nc.dram_tensor("ktm_l", [H, 16, CPR, 128], BF16)
    vtm_l = nc.dram_tensor("vtm_l", [H, 16, CPR, 128], BF16)
    kfm_g = nc.dram_tensor("kfm_g", [4, H, 128, TOK], BF16)
    ktm_g = nc.dram_tensor("ktm_g", [4, H, 16, CPR, 128], BF16)
    vtm_g = nc.dram_tensor("vtm_g", [4, H, 16, CPR, 128], BF16)
    z_l = nc.dram_tensor("z_l", [HL, 16, NCHUNK, 128], BF16)
    z_gt = nc.dram_tensor("z_gt", [4, HL, 16, NCHUNK, 128], BF16)
    routed_d = nc.dram_tensor("routed_d", [D, TOK], F32)
    shg_b = nc.dram_tensor("shg_b", [D, I_SH], BF16)
    shu_b = nc.dram_tensor("shu_b", [D, I_SH], BF16)
    shd_b = nc.dram_tensor("shd_b", [I_SH, D], BF16)

                
    dbg = {}
    if debug:
        dbg["xn"] = dT("dbg_xn", [128, KC * TOK], F32, kind="ExternalOutput")
        dbg["comb"] = dT("dbg_comb", [16, TOK], F32, kind="ExternalOutput")
        if max_phase >= 1:
            dbg["kfm"] = dT("dbg_kfm", [H, 128, TOK], BF16, kind="ExternalOutput")
            dbg["ktm"] = dT("dbg_ktm", [H, 16, CPR, 128], BF16, kind="ExternalOutput")
            dbg["vtm"] = dT("dbg_vtm", [H, 16, CPR, 128], BF16, kind="ExternalOutput")
        if max_phase >= 3:
            dbg["zl"] = dT("dbg_zl", [HL, 16, NCHUNK, 128], BF16, kind="ExternalOutput")
        if max_phase >= 5:
            dbg["routed"] = dT("dbg_routed", [D, TOK], F32, kind="ExternalOutput")

    ARENA_B = 132 * 1024
    arena_t = es.enter_context(nc.sbuf_tensor("arena", [128, ARENA_B], U8))
    ar = Arena(arena_t, ARENA_B)
    Sb = lambda n, sh, dt: es.enter_context(nc.sbuf_tensor(n, sh, dt))
    xn = Sb("xn", [128, KC * TOK], BF16)
    consts = Sb("consts", [128, 8], F32)
    ident_f = Sb("ident_f", [128, 128], F32)
    ident_b = Sb("ident_b", [128, 128], BF16)
    ropeP = Sb("ropeP", [128, 128], BF16)
    ones_r = Sb("ones_r", [1, 128], F32)
    stat = Sb("stat", [128, 8], F32)
    comb_tm = Sb("comb_tm", [128, NT * E], F32)
    comb_fm = Sb("comb_fm", [16, TOK], F32)
    rw_sb = Sb("rw_sb", [128, KC * E], BF16)
    sgw_sb = Sb("sgw_sb", [128, KC], BF16)

    Pb = lambda n, sh, dt: es.enter_context(nc.psum_tensor(n, sh, dt))
    p_a = Pb("p_a", [128, 512], F32)
    p_b = Pb("p_b", [128, 512], F32)
    p_c = Pb("p_c", [128, 512], F32)
    p_e = Pb("p_e", [128, 512], F32)
    p_f = Pb("p_f", [128, 512], F32)
    p_w0 = Pb("p_w0", [128, 128], F32)
    p_w1 = Pb("p_w1", [128, 128], F32)
    p_z = Pb("p_z", [128, 128], F32)
    p_wb = [p_w0, p_w1]
    pdn = [p_c, p_e]
    trb = [p_z, p_w0]

    c.sem_new("d_ms", "d_w", "d_cr",
              "d_zl", "d_xr", "d_out", "cc", "d_sc",
              "s_sq", "v_st", "s_sd", "v_g", "v_nm", "s_xn", "t_tr", "v_xn",
              "t_rt", "v_cb", "t_ct", "v_cf", "s_lq",
              "t_k", "v_k1", "t_pm", "v_k2", "t_tk", "v_k3", "t_v", "v_v",
              "t_z0", "t_z1", "v_dz0", "v_dz1", "s_zc0", "s_zc1",
              "t_g0", "t_g1", "s_w0", "s_w1",
              "t_bc", "v_cs", "t_mg", "t_mu", "s_si", "v_su", "v_te",
              "t_dn", "s_ev", "v_sh",
              "t_sg", "s_sg", "v_sg",
              "t_wo", "v_as", "t_ft", "v_ot", "v_ox", "d_dbg", "v_w0", "v_w1",
              "d_wk0", "d_wk1", "d_wv0", "d_wv1", "d_gu0", "d_gu1",
              "d_dn0", "d_dn1", "d_wo0", "d_wo1", "d_ac0", "d_ac1",
              "kf00", "kf01", "kf10", "kf11",
              "kc00", "kc01", "kc10", "kc11",
              "vc00", "vc01", "vc10", "vc11",
              "d_x0", "d_x1", "d_cs", "d_gb",
              "d_kr0", "d_kr1", "d_kt0", "d_kt1", "d_vt0", "d_vt1",
              "zw00", "zw01", "zw10", "zw11", "d_rs0", "d_rs1")

    with es, nc.Block() as block:
        x_tile = [ar.take(D, F32) for _ in range(2)]
        xsq = ar.take(D, F32)
        xn_tm = [ar.take(D, F32) for _ in range(2)]

        @block.sync
        def _(sy):
            c.dma(sy, "d_ms", consts[:], consts_in.ap())
            c.dma(sy, "d_ms", ident_f[:], ident_in.ap())
            c.dma(sy, "d_ms", ones_r[:], ones_in.ap())
            for T in range(NT):
                if T >= 2:
                    sy.wait_ge(sm["s_xn"], T - 1)
                c.dma(sy, f"d_x{T % 2}", x_tile[T % 2], x_in.ap()[T * 128:(T + 1) * 128, :])

        @block.gpsimd
        def _(g):
            c.dma(g, "d_w", ident_b[:], ident_in.ap())
            c.dma(g, "d_w", ropeP[:], ropeP_in.ap())
            c.dma(g, "d_w", rw_sb[:].rearrange("p (k e) -> p k e", k=KC),
                  rw_t.ap().rearrange("(k p) e -> p k e", p=128))
            c.dma(g, "d_w", sgw_sb[:], sgw_in.ap())

        MS_PRE = ct["d_ms"]

        @block.scalar
        def _(s):
            for T in range(NT):
                s.wait_ge(sm[f"d_x{T % 2}"], 16 * (T // 2 + 1))
                c.inc(s.square(xsq, x_tile[T % 2]), "s_sq", 1)
                s.wait_ge(sm["v_st"], T + 1)
                c.inc(s.activation(stat[:, 4:5], stat[:, 6:7], AF.Sqrt,
                                   bias=consts[:, C_EPS:C_EPS + 1]), "s_sd", 1)
                s.wait_ge(sm["v_nm"], T + 1)
                if T >= 2:
                    s.wait_ge(sm["t_tr"], 16 * (T - 1))
                c.inc(s.activation(xn_tm[T % 2], x_tile[T % 2], AF.Identity,
                                   bias=stat[:, 6:7], scale=stat[:, 5:6]), "s_xn", 1)

        @block.vector
        def _(v):
            v.wait_ge(sm["d_ms"], MS_PRE)
            for T in range(NT):
                v.wait_ge(sm[f"d_x{T % 2}"], 16 * (T // 2 + 1))
                def G(inst):
                    gvv = c.inc(inst, "v_g", 1)
                    v.wait_ge(sm["v_g"], gvv)
                G(v.reduce_sum(stat[:, 0:1], x_tile[T % 2], axis=AX))
                v.wait_ge(sm["s_sq"], T + 1)
                G(v.reduce_sum(stat[:, 1:2], xsq, axis=AX))
                G(v.tensor_scalar(stat[:, 2:3], stat[:, 0:1], consts[:, C_INVD:C_INVD + 1], None, ALU.mult))
                G(v.tensor_scalar(stat[:, 3:4], stat[:, 1:2], consts[:, C_INVD:C_INVD + 1], None, ALU.mult))
                G(v.tensor_mul(stat[:, 7:8], stat[:, 2:3], stat[:, 2:3]))
                c.inc(v.tensor_sub(stat[:, 6:7], stat[:, 3:4], stat[:, 7:8]), "v_st", 1)
                v.wait_ge(sm["s_sd"], T + 1)
                G(v.reciprocal(stat[:, 5:6], stat[:, 4:5]))
                c.inc(v.tensor_scalar(stat[:, 6:7], stat[:, 2:3], stat[:, 5:6],
                                      consts[:, C_NEG1:C_NEG1 + 1], ALU.mult, ALU.mult), "v_nm", 1)
                for k in range(KC):
                    if KSUB < 1:
                        break
                    v.wait_ge(sm["t_tr"], 16 * T + k + 1)
                    c.inc(v.tensor_copy(xn[:, k * TOK + T * 128: k * TOK + (T + 1) * 128],
                                        trb[k % 2][:, :]), "v_xn", 1)

        @block.tensor
        def _(t):
            t.wait_ge(sm["d_ms"], MS_PRE)
            t.wait_ge(sm["d_w"], 64)
            for T in range(NT):
                if KSUB < 1:
                    break
                t.wait_ge(sm["s_xn"], T + 1)
                for k in range(KC):
                    idx = 16 * T + k
                    if idx >= 1:
                        t.wait_ge(sm["v_xn"], idx - 1)
                    c.inc(t.transpose(trb[idx % 2][:, :],
                                      xn_tm[T % 2][:, k * 128:(k + 1) * 128], ident_f[:]), "t_tr", 1)

        probs = ar.take(E, F32)
        pn = ar.take(E, F32)
        wk2 = [ar.take(E, F32) for _ in range(2)]
        mk = ar.take(E, F32)
        cr = ar.take(E, F32)
        vg_bR = ct["v_g"]
        R_ON = KSUB >= 2

        @block.tensor
        def _(t):
            for T in range(NT):
                if not R_ON:
                    break
                t.wait_ge(sm["d_w"], 64)
                t.wait_ge(sm["v_xn"], 16 * T + 16)
                if T > 0:
                    t.wait_ge(sm["v_cb"], T)
                for k in range(KC):
                    ins = t.matmul(p_c[:, 0:E], xn[:, k * TOK + T * 128: k * TOK + (T + 1) * 128],
                                   rw_sb[:, k * E:(k + 1) * E], start=(k == 0), stop=(k == KC - 1))
                c.inc(ins, "t_rt", 1)
                t.wait_ge(sm["v_cb"], T + 1)
                if T > 0:
                    t.wait_ge(sm["v_cf"], T)
                c.inc(t.transpose(p_w1[0:16, :], comb_tm[:, T * E:(T + 1) * E], ident_f[:]), "t_ct", 1)

        @block.scalar
        def _(s):
            for T in range(NT):
                if not R_ON:
                    break
                s.wait_ge(sm["v_nm"], NT + T + 1)
                c.inc(s.activation(probs, p_c[:, 0:E], AF.Exp, bias=stat[:, 1:2]), "s_lq", 1)

        @block.vector
        def _(v):
            for T in range(NT):
                if not R_ON:
                    break
                v.wait_ge(sm["t_rt"], T + 1)
                def G(inst):
                    gvv = c.inc(inst, "v_g", 1)
                    v.wait_ge(sm["v_g"], gvv)
                G(v.reduce_max(stat[:, 0:1], p_c[:, 0:E], axis=AX))
                c.inc(v.tensor_scalar(stat[:, 1:2], stat[:, 0:1],
                                      consts[:, C_NEG1:C_NEG1 + 1], None, ALU.mult), "v_nm", 1)
                v.wait_ge(sm["s_lq"], T + 1)
                G(v.reduce_sum(stat[:, 2:3], probs, axis=AX))
                G(v.reciprocal(stat[:, 3:4], stat[:, 2:3]))
                G(v.tensor_scalar(pn, probs, stat[:, 3:4], None, ALU.mult))
                G(v.tensor_copy(wk2[0], pn))
                for j in range(3):
                    G(v.reduce_max(stat[:, 0:1], wk2[j % 2], axis=AX))
                    G(v.tensor_single_scalar(mk, wk2[j % 2], stat[:, 0:1], ALU.is_ge))
                    G(v.scalar_tensor_tensor(wk2[(j + 1) % 2], mk, consts[:, C_NBIG:C_NBIG + 1],
                                             wk2[j % 2], ALU.mult, ALU.add))
                G(v.reduce_max(stat[:, 0:1], wk2[1], axis=AX))
                G(v.tensor_single_scalar(mk, pn, stat[:, 0:1], ALU.is_ge))
                G(v.tensor_mul(cr, mk, pn))
                G(v.reduce_sum(stat[:, 1:2], cr, axis=AX))
                G(v.tensor_scalar(stat[:, 4:5], stat[:, 1:2], consts[:, C_TINY:C_TINY + 1], None, ALU.max))
                G(v.reciprocal(stat[:, 2:3], stat[:, 4:5]))
                c.inc(v.tensor_scalar(comb_tm[:, T * E:(T + 1) * E], cr, stat[:, 2:3], None, ALU.mult), "v_cb", 1)
                v.wait_ge(sm["t_ct"], T + 1)
                c.inc(v.tensor_copy(comb_fm[:, T * 128:(T + 1) * 128], p_w1[0:16, :]), "v_cf", 1)

        if max_phase >= 1:
            ar.reset()
            cos_f = ar.take(TOK, F32)
            sin_f = ar.take(TOK, F32)
            wproj = [ar.take(KC * 128, BF16) for _ in range(2)]
            wvproj = [ar.take(KC * 512, BF16) for _ in range(2)]
            kbf = ar.take(512, BF16)
            kra = ar.take(512, F32)
            krb = ar.take(512, F32)
            krope_f = [ar.take(512, F32) for _ in range(2)]
            krope = [ar.take(512, BF16) for _ in range(2)]
            ktmT = [ar.take(512, BF16) for _ in range(2)]
            vtm = [ar.take(512, BF16) for _ in range(2)]

            @block.sync
            def _(sy):
                sy.wait_ge(sm["s_xn"], NT)
                c.dma(sy, "d_cs", cos_f, cos_in.ap())
                c.dma(sy, "d_cs", sin_f, sin_in.ap())


            @block.gpsimd
            def _(g):
                g.wait_ge(sm["v_cf"], NT)
                for h in range(H):
                    if h >= 2:
                        g.wait_ge(sm["t_k"], (h - 1) * NTG)
                    c.dma(g, f"d_wk{h % 2}", wproj[h % 2].rearrange("p (k m) -> p k m", k=KC),
                          wk_t.ap()[:, h * 128:(h + 1) * 128].rearrange("(k p) m -> p k m", p=128))
                for eg in range(NTG):
                    if eg >= 2:
                        g.wait_ge(sm["t_v"], (eg - 1) * NT)
                    c.dma(g, f"d_wv{eg % 2}", wvproj[eg % 2].rearrange("p (k m) -> p k m", k=KC),
                          wv_t.ap()[:, eg * 512:(eg + 1) * 512].rearrange("(k p) m -> p k m", p=128))

            @block.tensor
            def _(t):
                t.wait_ge(sm["v_xn"], 16 * NT)
                t.wait_ge(sm["d_cs"], 32)
                t.wait_ge(sm["v_cf"], NT)
                for h in range(H):
                    t.wait_ge(sm[f"d_wk{h % 2}"], 16 * (h // 2 + 1))
                    for tg in range(NTG):
                        idx = h * NTG + tg
                        t.wait_ge(sm["v_k1"], idx)
                        for k in range(KC):
                            ins = t.matmul(p_a[:, :], wproj[h % 2][:, k * 128:(k + 1) * 128],
                                           xn[:, k * TOK + tg * 512: k * TOK + (tg + 1) * 512],
                                           start=(k == 0), stop=(k == KC - 1))
                        c.inc(ins, "t_k", 1)
                        t.wait_ge(sm["v_k1"], idx + 1)
                        t.wait_ge(sm["v_k2"], idx)
                        c.inc(t.matmul(p_b[:, :], ropeP[:], kbf, start=True, stop=True), "t_pm", 1)
                        t.wait_ge(sm["v_k2"], idx + 1)
                        for j in range(4):
                            tj = idx * 4 + j
                            if tj >= 2:
                                t.wait_ge(sm["v_k3"], tj - 1)
                            c.inc(t.transpose(trb[tj % 2][:, :],
                                              krope_f[idx % 2][:, j * 128:(j + 1) * 128], ident_f[:]),
                                  "t_tk", 1)
                for eg in range(NTG):
                    t.wait_ge(sm[f"d_wv{eg % 2}"], 16 * (eg // 2 + 1))
                    for T in range(NT):
                        idxv = eg * NT + T
                        if idxv == 0:
                            t.wait_ge(sm["v_k1"], H * NTG)
                        t.wait_ge(sm["v_v"], idxv)
                        for k in range(KC):
                            ins = t.matmul(p_a[:, :], xn[:, k * TOK + T * 128: k * TOK + (T + 1) * 128],
                                           wvproj[eg % 2][:, k * 512:(k + 1) * 512],
                                           start=(k == 0), stop=(k == KC - 1))
                        c.inc(ins, "t_v", 1)

            @block.vector
            def _(v):
                v.wait_ge(sm["d_cs"], 32)
                for h in range(H):
                    for tg in range(NTG):
                        idx = h * NTG + tg
                        v.wait_ge(sm["t_k"], idx + 1)
                        c.inc(v.tensor_copy(kbf, p_a[:, :]), "v_k1", 1)
                        v.wait_ge(sm["t_pm"], idx + 1)
                        if idx >= 2:
                            v.wait_ge(sm[f"d_kr{idx % 2}"], 16 * (idx // 2))
                            v.wait_ge(sm["t_tk"], 4 * (idx - 1))
                        v.tensor_mul(kra, kbf, cos_f[:, tg * 512:(tg + 1) * 512])
                        v.tensor_mul(krb, p_b[:, :], sin_f[:, tg * 512:(tg + 1) * 512])
                        v.tensor_add(krope_f[idx % 2], kra, krb)
                        c.inc(v.tensor_copy(krope[idx % 2], krope_f[idx % 2]), "v_k2", 1)
                        if idx >= 2:
                            v.wait_ge(sm[f"d_kt{idx % 2}"], 64 * (idx // 2))
                        for j in range(4):
                            tj = idx * 4 + j
                            v.wait_ge(sm["t_tk"], tj + 1)
                            c.inc(v.tensor_copy(ktmT[idx % 2][:, j * 128:(j + 1) * 128],
                                                trb[tj % 2][:, :]), "v_k3", 1)
                for eg in range(NTG):
                    for T in range(NT):
                        idxv = eg * NT + T
                        v.wait_ge(sm["t_v"], idxv + 1)
                        if idxv >= 2:
                            v.wait_ge(sm[f"d_vt{idxv % 2}"], 64 * (idxv // 2))
                        c.inc(v.tensor_scalar(vtm[idxv % 2], p_a[:, :],
                                              consts[:, C_INVN:C_INVN + 1], None, ALU.mult), "v_v", 1)

            @block.sync
            def _(sy):
                for h in range(H):
                    for tg in range(NTG):
                        idx = h * NTG + tg
                        sy.wait_ge(sm["v_k2"], idx + 1)
                        c.dma(sy, f"d_kr{idx % 2}", kfm_l.ap()[h, :, tg * 512:(tg + 1) * 512],
                              krope[idx % 2])
                        sy.wait_ge(sm["v_k3"], 4 * (idx + 1))
                        for sub in range(4):
                            c0 = tg * 32 + sub * 8
                            c.dma(sy, f"d_kt{idx % 2}",
                                  ktm_l.ap()[h, :, c0:c0 + 8, :].rearrange("p c d -> c p d"),
                                  ktmT[idx % 2][:, sub * 128:(sub + 1) * 128])
                for eg in range(NTG):
                    for T in range(NT):
                        idxv = eg * NT + T
                        sy.wait_ge(sm["v_v"], idxv + 1)
                        for hh in range(4):
                            c.dma(sy, f"d_vt{idxv % 2}",
                                  vtm_l.ap()[4 * eg + hh, :, T * 8:(T + 1) * 8, :].rearrange("p c e -> c p e"),
                                  vtm[idxv % 2][:, hh * 128:(hh + 1) * 128])

        if max_phase >= 5:
            @block.gpsimd
            def _(g):
                for i in range(32):
                    c.inc(g.dma_start(out=shg_b.ap()[i * 64:(i + 1) * 64, :].rearrange("a b -> (a b)"),
                                      in_=shg_in.ap()[i * 64:(i + 1) * 64, :].rearrange("a b -> (a b)")), "d_sc", 16)
                    c.inc(g.dma_start(out=shu_b.ap()[i * 64:(i + 1) * 64, :].rearrange("a b -> (a b)"),
                                      in_=shu_in.ap()[i * 64:(i + 1) * 64, :].rearrange("a b -> (a b)")), "d_sc", 16)
                    c.inc(g.dma_start(out=shd_b.ap()[i * 176:(i + 1) * 176, :].rearrange("a b -> (a b)"),
                                      in_=shd_in.ap()[i * 176:(i + 1) * 176, :].rearrange("a b -> (a b)")), "d_sc", 16)

        if max_phase >= 2:
            @block.gpsimd
            def _(g):
                for nm in ("d_kr0", "d_kr1", "d_kt0", "d_kt1", "d_vt0", "d_vt1"):
                    g.wait_ge(sm[nm], ct[nm])
                c.inc(g.collective_compute(
                    "AllGather", ALU.bypass, replica_groups=[[0, 1, 2, 3], [4, 5, 6, 7]],
                    ins=[kfm_l.ap().rearrange("h d t -> (h d) t")],
                    outs=[kfm_g.ap().rearrange("r h d t -> (r h d) t")]), "cc", 1)
                c.inc(g.collective_compute(
                    "AllGather", ALU.bypass, replica_groups=[[0, 1, 2, 3], [4, 5, 6, 7]],
                    ins=[ktm_l.ap().rearrange("h p c d -> (h p) (c d)")],
                    outs=[ktm_g.ap().rearrange("r h p c d -> (r h p) (c d)")]), "cc", 1)
                c.inc(g.collective_compute(
                    "AllGather", ALU.bypass, replica_groups=[[0, 1, 2, 3], [4, 5, 6, 7]],
                    ins=[vtm_l.ap().rearrange("h p c d -> (h p) (c d)")],
                    outs=[vtm_g.ap().rearrange("r h p c d -> (r h p) (c d)")]), "cc", 1)

        if max_phase >= 3:
            ar.reset()
            kfm_sb = [[ar.take(TOK, BF16) for _ in range(2)] for _ in range(2)]
            kcs_sb = [[ar.take(16 * 128, BF16)[0:16, :] for _ in range(2)] for _ in range(2)]
            vcs_sb = [[ar.take(16 * 128, BF16)[0:16, :] for _ in range(2)] for _ in range(2)]
            z_sub = [[ar.take(16 * 128, BF16)[0:16, :] for _ in range(2)] for _ in range(2)]
            dz_sb = [ar.take(128, BF16)[0:16, :] for _ in range(2)]
            wsb = [ar.take(128, BF16) for _ in range(2)]
            wf = [[ar.take(128, F32) for _ in range(2)] for _ in range(2)]
            zb = [p_z, p_a]
            ninv16 = consts[0:16, C_NINV:C_NINV + 1]

            for half in range(2):
                tz_b = {s: ct[f"t_z{s}"] for s in range(2)}
                tg_b = {s: ct[f"t_g{s}"] for s in range(2)}
                vd_b = {s: ct[f"v_dz{s}"] for s in range(2)}
                sw_b = {s: ct[f"s_w{s}"] for s in range(2)}
                vw_b = {s: ct[f"v_w{s}"] for s in range(2)}
                zc_b = {s: ct[f"s_zc{s}"] for s in range(2)}
                zwb_sl = {(s, j): ct[f"zw{s}{j}"] for s in range(2) for j in range(2)}
                kfm_cnt, kcs_cnt, vcs_cnt = {}, {}, {}

                @block.gpsimd
                def _(g, half=half, tz_b=tz_b, tg_b=tg_b, vd_b=vd_b):
                    if half == 0:
                        g.wait_ge(sm["cc"], 3)
                        g.wait_ge(sm["v_v"], ct["v_v"])

                    def body(lc):
                        b_, q_ = lc // 4, lc % 4
                        for ss in range((KSCAN + 15) // 16):
                            r = ss // 8
                            if ss % 8 == 0:
                                for s in range(2):
                                    hl = half * 2 + s
                                    h_ = 4 * q_ + hl
                                    g.wait_ge(sm[f"t_z{s}"], tz_b[s] + max(0, r - 1) * CPR)
                                    kfm_cnt[(s, r)] = c.dma(
                                        g, f"kf{s}{r % 2}", kfm_sb[s][r % 2],
                                        kfm_g.ap()[r, h_, :, :])
                            for s in range(2):
                                hl = half * 2 + s
                                h_ = 4 * q_ + hl
                                g.wait_ge(sm[f"t_g{s}"], tg_b[s] + ((ss - 1) * 16 if ss >= 2 else 0))
                                kcs_cnt[(s, ss)] = c.dma(
                                    g, f"kc{s}{ss % 2}", kcs_sb[s][ss % 2],
                                    ktm_g.ap()[r, h_, :, (ss % 8) * 16:(ss % 8) * 16 + 16, :]
                                    .rearrange("p c d -> p (c d)"))
                                g.wait_ge(sm[f"v_dz{s}"], vd_b[s] + ((ss - 1) * 16 if ss >= 2 else 0))
                                vcs_cnt[(s, ss)] = c.dma(
                                    g, f"vc{s}{ss % 2}", vcs_sb[s][ss % 2],
                                    vtm_g.ap()[r, h_, :, (ss % 8) * 16:(ss % 8) * 16 + 16, :]
                                    .rearrange("p c d -> p (c d)"))
                    c.core_switch(g, body)

                @block.tensor
                def _(t, half=half, tz_b=tz_b, tg_b=tg_b, vd_b=vd_b, sw_b=sw_b, zc_b=zc_b,
                      kfm_cnt=kfm_cnt, kcs_cnt=kcs_cnt):
                    if half == 0:
                        t.wait_ge(sm["v_v"], ct["v_v"])
                    for i in range(KSCAN):
                        r, cl = i // CPR, i % CPR
                        ss, cs = i // 16, i % 16
                        for s in range(2):
                            t.wait_ge(sm[f"s_w{s}"], sw_b[s] + i + 1)
                            t.wait_ge(sm[f"kf{s}{r % 2}"], kfm_cnt[(s, r)])
                            t.wait_ge(sm[f"v_dz{s}"], vd_b[s] + i)
                            if KZ:
                                t.wait_ge(sm[f"s_zc{s}"], zc_b[s] + i)
                            c.inc(t.matmul(zb[s][0:16, 0:128], kfm_sb[s][r % 2][:, cl * 16:(cl + 1) * 16],
                                           wsb[s], start=True, stop=True), f"t_z{s}", 1)
                        for s in range(2):
                            t.wait_ge(sm[f"v_dz{s}"], vd_b[s] + i + 1)
                            t.wait_ge(sm[f"kc{s}{ss % 2}"], kcs_cnt[(s, ss)])
                            t.wait_ge(sm[f"v_w{s}"], vw_b[s] + i)
                            c.inc(t.matmul(p_wb[s][:, :], kcs_sb[s][ss % 2][:, cs * 128:(cs + 1) * 128],
                                           dz_sb[s], start=True, stop=True), f"t_g{s}", 1)

                @block.vector
                def _(v, half=half, tz_b=tz_b, tg_b=tg_b, vd_b=vd_b, vw_b=vw_b, sw_b=sw_b, vcs_cnt=vcs_cnt):
                    if half == 0:
                        for nm in ("d_kt0", "d_kt1", "d_vt0", "d_vt1"):
                            v.wait_ge(sm[nm], ct[nm])
                    for s in range(2):
                        v.tensor_copy(wf[s][0], ident_f[:])
                    for i in range(KSCAN):
                        ss, cs = i // 16, i % 16
                        for s in range(2):
                            v.wait_ge(sm[f"t_z{s}"], tz_b[s] + i + 1)
                            v.wait_ge(sm[f"vc{s}{ss % 2}"], vcs_cnt[(s, ss)])
                            v.wait_ge(sm[f"t_g{s}"], tg_b[s] + i)
                            c.inc(v.scalar_tensor_tensor(dz_sb[s], zb[s][0:16, 0:128], ninv16,
                                                         vcs_sb[s][ss % 2][:, cs * 128:(cs + 1) * 128],
                                                         ALU.mult, ALU.add), f"v_dz{s}", 1)
                        for s in range(2):
                            v.wait_ge(sm[f"t_g{s}"], tg_b[s] + i + 1)
                            v.wait_ge(sm[f"s_w{s}"], sw_b[s] + i)
                            c.inc(v.tensor_add(wf[s][(i + 1) % 2], wf[s][i % 2], p_wb[s][:, :]), f"v_w{s}", 1)

                @block.sync
                def _(sy, half=half, zc_b=zc_b):
                    for ss in range(KSCAN // 16 if KZ == 1 else 0):
                        for s in range(2):
                            hl = half * 2 + s
                            sy.wait_ge(sm[f"s_zc{s}"], zc_b[s] + (ss + 1) * 16)
                            c.dma(sy, f"zw{s}{ss % 2}",
                                  z_l.ap()[hl, :, ss * 16:(ss + 1) * 16, :].rearrange("p c e -> p (c e)"),
                                  z_sub[s][ss % 2])

                @block.scalar
                def _(s_, half=half, tz_b=tz_b, tg_b=tg_b, zc_b=zc_b, sw_b=sw_b,
                      vw_b=vw_b, vd_b=vd_b, zwb_sl=zwb_sl):
                    if half == 0:
                        for nm in ("d_kt0", "d_kt1", "d_vt0", "d_vt1"):
                            s_.wait_ge(sm[nm], ct[nm])
                    for s in range(2):
                        if half == 1:
                            s_.wait_ge(sm[f"t_z{s}"], tz_b[s])
                        c.inc(s_.copy(wsb[s], ident_b[:]), f"s_w{s}", 1)
                    for i in range(KSCAN):
                        ss, cs = i // 16, i % 16
                        for s in range(2):
                            hl = half * 2 + s
                            if KZ:
                                s_.wait_ge(sm[f"t_z{s}"], tz_b[s] + i + 1)
                                s_.wait_ge(sm[f"v_dz{s}"], vd_b[s] + i + 1)
                                if cs == 0 and ss >= 2 and KZ == 1:
                                    s_.wait_ge(sm[f"zw{s}{ss % 2}"], zwb_sl[(s, ss % 2)] + 16 * (ss // 2))
                                c.inc(s_.copy(z_sub[s][ss % 2][:, cs * 128:(cs + 1) * 128],
                                              zb[s][0:16, 0:128]), f"s_zc{s}", 1)
                            s_.wait_ge(sm[f"v_w{s}"], vw_b[s] + i + 1)
                            c.inc(s_.copy(wsb[s], wf[s][(i + 1) % 2]), f"s_w{s}", 1)

        if max_phase >= 5:
            ar.reset()
            wgu = [ar.take(KC * 2 * 128, BF16) for _ in range(2)]
            wdn = [ar.take(MC * 128, BF16) for _ in range(2)]
            t_e = ar.take(MC * TOK, BF16)
            silu_g = [ar.take(512, F32) for _ in range(2)]
            su = ar.take(512, F32)
            devac_w = [ar.take(TOK, F32) for _ in range(2)]
            m5 = ar.mark()
            comb_row = ar.take(TOK, F32)[0:1, :]
            comb_sb = ar.take(TOK, F32)
            pga = [p_a, p_c]
            pgu = [p_b, p_f]
            SCAN_DONE = {k: ct[k] for k in ("v_w0", "v_w1", "v_dz0", "v_dz1", "s_zc0", "s_zc1", "s_w0", "s_w1")}

            @block.gpsimd
            def _(g):
                g.wait_ge(sm["v_cf"], NT)
                g.wait_ge(sm["s_w0"], SCAN_DONE["s_w0"])
                g.wait_ge(sm["s_w1"], SCAN_DONE["s_w1"])
                for s in range(2):
                    for j in range(2):
                        g.wait_ge(sm[f"zw{s}{j}"], ct[f"zw{s}{j}"])
                for e in range(E):
                    g.wait_ge(sm["t_bc"], 4 * e)
                    c.dma(g, "d_cr", comb_row, comb_fm[e:e + 1, :])
                    for m in range(MC):
                        gi = e * MC + m
                        if gi >= 2:
                            g.wait_ge(sm["t_mu"], (gi - 1) * 4)
                        vw = wgu[gi % 2].rearrange("p (k g m) -> p k g m", k=KC, g=2)
                        c.dma(g, f"d_gu{gi % 2}", vw[:, :, 0, :],
                              wg_in.ap()[e, :, m * 128:(m + 1) * 128].rearrange("(k p) m -> p k m", p=128))
                        c.dma(g, f"d_gu{gi % 2}", vw[:, :, 1, :],
                              wu_in.ap()[e, :, m * 128:(m + 1) * 128].rearrange("(k p) m -> p k m", p=128))
                        if gi == 1 and max_phase >= 4:
                            c.inc(g.collective_compute(
                                "AllGather", ALU.bypass, replica_groups=[[0, 1, 2, 3], [4, 5, 6, 7]],
                                ins=[z_l.ap().rearrange("l p c e -> (l p) (c e)")],
                                outs=[z_gt.ap().rearrange("r l p c e -> (r l p) (c e)")]), "cc", 1)
                    for o in range(E + 1):
                        if o < E:
                            oi = e * E + o
                            if oi >= 2:
                                g.wait_ge(sm["t_dn"], (oi - 1) * 4)
                            c.dma(g, f"d_dn{oi % 2}", wdn[oi % 2].rearrange("p (k m) -> p k m", k=MC),
                                  wd_in.ap()[e, :, o * 128:(o + 1) * 128].rearrange("(k p) m -> p k m", p=128))
                        if o >= 1:
                            oi = e * E + (o - 1)
                            g.wait_ge(sm["s_ev"], 4 * (oi + 1))
                            c.dma(g, f"d_ac{oi % 2}",
                                  routed_d.ap()[(o - 1) * 128:o * 128, :],
                                  devac_w[oi % 2],
                                  accum_op=(ALU.add if e > 0 else ALU.bypass))

            @block.tensor
            def _(t):
                for k in ("v_w0", "v_w1", "v_dz0", "v_dz1", "s_zc0", "s_zc1"):
                    t.wait_ge(sm[k], SCAN_DONE[k])
                for e in range(E):
                    t.wait_ge(sm["d_cr"], 16 * (e + 1))
                    for tg in range(NTG):
                        if e > 0:
                            t.wait_ge(sm["s_ev"], e * 64)
                        t.wait_ge(sm["v_cs"], 4 * e + tg)
                        c.inc(t.matmul(p_e[:, :], ones_r[:], comb_row[:, tg * 512:(tg + 1) * 512],
                                       start=True, stop=True), "t_bc", 1)
                    for m in range(MC):
                        gi = e * MC + m
                        t.wait_ge(sm[f"d_gu{gi % 2}"], 32 * (gi // 2 + 1))
                        for tg in range(NTG):
                            gseq = gi * 4 + tg
                            if gseq >= 2:
                                t.wait_ge(sm["s_si"], gseq - 1)
                            for k in range(KC):
                                ins = t.matmul(pga[gseq % 2][:, :], wgu[gi % 2][:, (k * 2) * 128:(k * 2 + 1) * 128],
                                               xn[:, k * TOK + tg * 512: k * TOK + (tg + 1) * 512],
                                               start=(k == 0), stop=(k == KC - 1))
                            c.inc(ins, "t_mg", 1)
                            if gseq >= 2:
                                t.wait_ge(sm["v_su"], gseq - 1)
                            for k in range(KC):
                                ins = t.matmul(pgu[gseq % 2][:, :], wgu[gi % 2][:, (k * 2 + 1) * 128:(k * 2 + 2) * 128],
                                               xn[:, k * TOK + tg * 512: k * TOK + (tg + 1) * 512],
                                               start=(k == 0), stop=(k == KC - 1))
                            c.inc(ins, "t_mu", 1)
                    for o in range(E):
                        oi = e * E + o
                        t.wait_ge(sm[f"d_dn{oi % 2}"], 16 * (oi // 2 + 1))
                        for tg in range(NTG):
                            di = oi * NTG + tg
                            if o == 0 and tg == 0:
                                t.wait_ge(sm["v_te"], (e + 1) * 44)
                                t.wait_ge(sm["v_cs"], 4 * (e + 1))
                            t.wait_ge(sm["s_ev"], di)
                            for k in range(MC):
                                ins = t.matmul(p_e[:, :], wdn[oi % 2][:, k * 128:(k + 1) * 128],
                                               t_e[:, k * TOK + tg * 512: k * TOK + (tg + 1) * 512],
                                               start=(k == 0), stop=(k == MC - 1))
                            c.inc(ins, "t_dn", 1)

            @block.scalar
            def _(s):
                for e in range(E):
                    for m in range(MC):
                        for tg in range(NTG):
                            gseq = (e * MC + m) * 4 + tg
                            s.wait_ge(sm["t_mg"], gseq + 1)
                            if gseq >= 2:
                                s.wait_ge(sm["v_su"], gseq - 1)
                            c.inc(s.activation(silu_g[gseq % 2], pga[gseq % 2][:, :], AF.Silu), "s_si", 1)
                    for o in range(E):
                        for tg in range(NTG):
                            di = (e * E + o) * NTG + tg
                            oi = e * E + o
                            s.wait_ge(sm["t_dn"], di + 1)
                            if tg == 0 and oi >= 2:
                                s.wait_ge(sm[f"d_ac{oi % 2}"], 16 * (oi // 2))
                            c.inc(s.copy(devac_w[oi % 2][:, tg * 512:(tg + 1) * 512], p_e[:, :]), "s_ev", 1)

            @block.vector
            def _(v):
                for e in range(E):
                    for tg in range(NTG):
                        v.wait_ge(sm["t_bc"], 4 * e + tg + 1)
                        c.inc(v.tensor_copy(comb_sb[:, tg * 512:(tg + 1) * 512], p_e[:, :]), "v_cs", 1)
                    for m in range(MC):
                        for tg in range(NTG):
                            gseq = (e * MC + m) * 4 + tg
                            v.wait_ge(sm["s_si"], gseq + 1)
                            v.wait_ge(sm["t_mu"], gseq + 1)
                            c.inc(v.tensor_mul(su, silu_g[gseq % 2], pgu[gseq % 2][:, :]), "v_su", 1)
                            if m == 0 and tg == 0 and e > 0:
                                v.wait_ge(sm["t_dn"], e * 64)
                            c.inc(v.tensor_mul(t_e[:, m * TOK + tg * 512:m * TOK + (tg + 1) * 512],
                                               su, comb_sb[:, tg * 512:(tg + 1) * 512]), "v_te", 1)

        if max_phase >= 5:
            ar.rollback(m5)
            sg_row = ar.take(TOK, F32)[0:1, :]
            sg_bc = ar.take(TOK, F32)
            shd_big = [ar.take(SHC * 128, BF16) for _ in range(2)]
            devac = [ar.take(512, F32) for _ in range(2)]
            t_sh = t_e
            gu_b = {0: ct["d_gu0"], 1: ct["d_gu1"]}
            dn_b = {0: ct["d_dn0"], 1: ct["d_dn1"]}
            ac_b = {0: ct["d_ac0"], 1: ct["d_ac1"]}
            tdn_b = ct["t_dn"]
            tmg_b = ct["t_mg"]
            tmu_b = ct["t_mu"]
            ssi_b = ct["s_si"]
            vte_b = ct["v_te"]
            VCS_ALL = ct["v_cs"]
            SEV_ALL = ct["s_ev"]
            SSI_ALL = ct["s_si"]
            VSU_ALL = ct["v_su"]

            @block.tensor
            def _(t):
                t.wait_ge(sm["d_w"], 64)
                t.wait_ge(sm["s_ev"], SEV_ALL)
                t.wait_ge(sm["v_cs"], VCS_ALL)
                t.wait_ge(sm["s_si"], SSI_ALL)
                t.wait_ge(sm["v_su"], VSU_ALL)
                for tg in range(NTG):
                    t.wait_ge(sm["v_sg"], 2 * tg)
                    for k in range(KC):
                        ins = t.matmul(p_e[0:1, :], sgw_sb[:, k:k + 1],
                                       xn[:, k * TOK + tg * 512: k * TOK + (tg + 1) * 512],
                                       start=(k == 0), stop=(k == KC - 1))
                    c.inc(ins, "t_sg", 1)
                    t.wait_ge(sm["s_sg"], tg + 1)
                    c.inc(t.matmul(p_e[:, :], ones_r[:], sg_row[:, tg * 512:(tg + 1) * 512],
                                   start=True, stop=True), "t_sg", 1)
                for tg in range(NTG):
                    for ic in range(SHC):
                        gi = tg * SHC + ic
                        t.wait_ge(sm[f"d_gu{gi % 2}"], gu_b[gi % 2] + 32 * (gi // 2 + 1))
                        t.wait_ge(sm["s_si"], ssi_b + max(0, gi - 1))
                        for k in range(KC):
                            ins = t.matmul(pga[gi % 2][:, :], wgu[gi % 2][:, (k * 2) * 128:(k * 2 + 1) * 128],
                                           xn[:, k * TOK + tg * 512: k * TOK + (tg + 1) * 512],
                                           start=(k == 0), stop=(k == KC - 1))
                        c.inc(ins, "t_mg", 1)
                        t.wait_ge(sm["v_te"], vte_b + max(0, gi - 1))
                        for k in range(KC):
                            ins = t.matmul(pgu[gi % 2][:, :], wgu[gi % 2][:, (k * 2 + 1) * 128:(k * 2 + 2) * 128],
                                           xn[:, k * TOK + tg * 512: k * TOK + (tg + 1) * 512],
                                           start=(k == 0), stop=(k == KC - 1))
                        c.inc(ins, "t_mu", 1)
                    for o in range(E):
                        oi = tg * E + o
                        t.wait_ge(sm[f"d_dn{oi % 2}"], dn_b[oi % 2] + 16 * (oi // 2 + 1))
                        if oi == 0:
                            t.wait_ge(sm["v_sg"], 8)
                        if o == 0:
                            t.wait_ge(sm["v_te"], vte_b + (tg + 1) * SHC)
                        t.wait_ge(sm["v_sh"], oi)
                        for k in range(SHC):
                            ins = t.matmul(p_e[:, :], shd_big[oi % 2][:, k * 128:(k + 1) * 128],
                                           t_sh[:, k * 512:(k + 1) * 512],
                                           start=(k == 0), stop=(k == SHC - 1))
                        c.inc(ins, "t_dn", 1)

            @block.scalar
            def _(s):
                for tg in range(NTG):
                    s.wait_ge(sm["t_sg"], 2 * tg + 1)
                    c.inc(s.activation(sg_row[:, tg * 512:(tg + 1) * 512], p_e[0:1, :], AF.Sigmoid), "s_sg", 1)
                for tg in range(NTG):
                    for ic in range(SHC):
                        gi = tg * SHC + ic
                        s.wait_ge(sm["t_mg"], tmg_b + gi + 1)
                        if gi >= 2:
                            s.wait_ge(sm["v_te"], vte_b + gi - 1)
                        c.inc(s.activation(silu_g[gi % 2], pga[gi % 2][:, :], AF.Silu), "s_si", 1)

            @block.vector
            def _(v):
                for tg in range(NTG):
                    v.wait_ge(sm["t_sg"], 2 * tg + 2)
                    c.inc(v.tensor_copy(sg_bc[:, tg * 512:(tg + 1) * 512], p_e[:, :]), "v_sg", 2)
                for tg in range(NTG):
                    for ic in range(SHC):
                        gi = tg * SHC + ic
                        v.wait_ge(sm["s_si"], ssi_b + gi + 1)
                        v.wait_ge(sm["t_mu"], tmu_b + gi + 1)
                        if ic == 0 and tg > 0:
                            v.wait_ge(sm["t_dn"], tdn_b + tg * E)
                        c.inc(v.tensor_mul(t_sh[:, ic * 512:(ic + 1) * 512], silu_g[gi % 2],
                                           pgu[gi % 2][:, :]), "v_te", 1)
                    for o in range(E):
                        oi = tg * E + o
                        v.wait_ge(sm["t_dn"], tdn_b + oi + 1)
                        if oi >= 2:
                            v.wait_ge(sm[f"d_ac{oi % 2}"], ac_b[oi % 2] + 16 * (oi // 2))
                        c.inc(v.tensor_mul(devac[oi % 2], p_e[:, :],
                                           sg_bc[:, tg * 512:(tg + 1) * 512]), "v_sh", 1)

            @block.gpsimd
            def _(g):
                g.wait_ge(sm["d_sc"], ct["d_sc"])
                for tg in range(NTG):
                    for ic in range(SHC):
                        gi = tg * SHC + ic
                        g.wait_ge(sm["t_mu"], tmu_b + max(0, gi - 1))
                        vw = wgu[gi % 2].rearrange("p (k g m) -> p k g m", k=KC, g=2)
                        c.dma(g, f"d_gu{gi % 2}", vw[:, :, 0, :],
                              shg_b.ap()[:, ic * 128:(ic + 1) * 128].rearrange("(k p) m -> p k m", p=128))
                        c.dma(g, f"d_gu{gi % 2}", vw[:, :, 1, :],
                              shu_b.ap()[:, ic * 128:(ic + 1) * 128].rearrange("(k p) m -> p k m", p=128))
                    for o in range(E):
                        oi = tg * E + o
                        if oi >= 2:
                            g.wait_ge(sm["t_dn"], tdn_b + oi - 1)
                        c.dma(g, f"d_dn{oi % 2}", shd_big[oi % 2].rearrange("p (k m) -> p k m", k=SHC),
                              shd_b.ap()[:, o * 128:(o + 1) * 128].rearrange("(k p) m -> p k m", p=128))
                        g.wait_ge(sm["v_sh"], oi + 1)
                        c.dma(g, f"d_ac{oi % 2}",
                              routed_d.ap()[o * 128:(o + 1) * 128, tg * 512:(tg + 1) * 512],
                              devac[oi % 2], accum_op=ALU.add)

        if max_phase >= 6:
            ar.reset()
            gam_b = ar.take(D, F32)
            bet_b = ar.take(D, F32)
            z_tile = ar.take(D, BF16)
            lnsc = ar.take(D, F32)
            lnz_f = ar.take(D, F32)
            lnfin = ar.take(D, F32)
            lnz_fm = ar.take(4 * D, BF16)
            wo_sb = [ar.take(KC * 128, BF16) for _ in range(2)]
            rsl = [ar.take(512, F32) for _ in range(2)]
            asm1 = ar.take(512, F32)
            out_tm = [ar.take(D, F32) for _ in range(4)]
            p_zf = [p_w1[:, :], p_c[:, 0:128]]
            pab = [p_a, p_b]

            b6 = {k: ct[k] for k in ("s_sq", "v_st", "s_sd", "v_nm", "s_xn",
                                     "t_tr", "v_xn", "d_zl", "v_g")}
            AC0, AC1 = ct["d_ac0"], ct["d_ac1"]

            @block.sync
            def _(sy):
                sy.wait_ge(sm["d_ac0"], AC0)
                sy.wait_ge(sm["d_ac1"], AC1)
                c.dma(sy, "d_gb", gam_b, gam_in.ap())
                c.dma(sy, "d_gb", bet_b, bet_in.ap())
                for tg in range(NTG):
                    for o in range(E):
                        woi = tg * E + o
                        if woi >= 2:
                            sy.wait_ge(sm["v_as"], woi - 1)
                        c.dma(sy, f"d_rs{woi % 2}", rsl[woi % 2],
                              routed_d.ap()[o * 128:(o + 1) * 128, tg * 512:(tg + 1) * 512])


            @block.gpsimd
            def _(g):
                g.wait_ge(sm["cc"], ct["cc"])
                g.wait_ge(sm["d_ac0"], AC0)
                g.wait_ge(sm["d_ac1"], AC1)
                for T in range(NT):
                    c.dma(g, "d_out", out_t.ap()[T * 128:(T + 1) * 128, :],
                          x_in.ap()[T * 128:(T + 1) * 128, :])
                for tg in range(NTG):
                    for jj in range(4):
                        T = tg * 4 + jj
                        if T > 0:
                            g.wait_ge(sm["s_xn"], b6["s_xn"] + T)
                        g.wait_ge(sm["d_zl"], b6["d_zl"] + T * 256)

                        def bodyz(lc, T=T):
                            b_, q_ = lc // 4, lc % 4
                            c0 = q_ * CPR + T * 8
                            for rh in range(4):
                                for hl in range(HL):
                                    c.inc(g.dma_start(
                                        out=z_tile[:, (rh * 4 + hl) * 128:(rh * 4 + hl + 1) * 128],
                                        in_=z_gt.ap()[rh, hl, :, c0:c0 + 8, :]
                                        .rearrange("p c e -> c p e")), "d_zl", 16)
                        c.core_switch(g, bodyz)
                    for o in range(E):
                        woi = tg * E + o
                        if woi >= 2:
                            g.wait_ge(sm["t_wo"], woi - 1)
                        c.dma(g, f"d_wo{woi % 2}", wo_sb[woi % 2].rearrange("p (k m) -> p k m", k=KC),
                              wo_t.ap()[:, o * 128:(o + 1) * 128].rearrange("(k p) m -> p k m", p=128))
                    for jj in range(4):
                        T = tg * 4 + jj
                        g.wait_ge(sm["v_ot"], 64 * (tg + 1))
                        c.dma(g, "d_out", out_t.ap()[T * 128:(T + 1) * 128, :],
                              out_tm[jj], accum_op=ALU.add)

            @block.scalar
            def _(s):
                for T in range(NT):
                    s.wait_ge(sm["d_zl"], b6["d_zl"] + (T + 1) * 256)
                    c.inc(s.square(lnsc, z_tile), "s_sq", 1)
                    s.wait_ge(sm["v_st"], b6["v_st"] + T + 1)
                    c.inc(s.activation(stat[:, 4:5], stat[:, 6:7], AF.Sqrt,
                                       bias=consts[:, C_EPS:C_EPS + 1]), "s_sd", 1)
                    s.wait_ge(sm["v_nm"], b6["v_nm"] + T + 1)
                    c.inc(s.activation(lnsc, z_tile, AF.Identity,
                                       bias=stat[:, 6:7], scale=stat[:, 5:6]), "s_xn", 1)

            @block.vector
            def _(v):
                v.wait_ge(sm["d_gb"], 32)
                for tg in range(NTG):
                    for jj in range(4):
                        T = tg * 4 + jj
                        v.wait_ge(sm["d_zl"], b6["d_zl"] + (T + 1) * 256)
                        def G(inst):
                            gvv = c.inc(inst, "v_g", 1)
                            v.wait_ge(sm["v_g"], gvv)
                        G(v.reduce_sum(stat[:, 0:1], z_tile, axis=AX))
                        v.wait_ge(sm["s_sq"], b6["s_sq"] + T + 1)
                        G(v.reduce_sum(stat[:, 1:2], lnsc, axis=AX))
                        G(v.tensor_scalar(stat[:, 2:3], stat[:, 0:1], consts[:, C_INVD:C_INVD + 1], None, ALU.mult))
                        G(v.tensor_scalar(stat[:, 3:4], stat[:, 1:2], consts[:, C_INVD:C_INVD + 1], None, ALU.mult))
                        G(v.tensor_mul(stat[:, 7:8], stat[:, 2:3], stat[:, 2:3]))
                        c.inc(v.tensor_sub(stat[:, 6:7], stat[:, 3:4], stat[:, 7:8]), "v_st", 1)
                        v.wait_ge(sm["s_sd"], b6["s_sd"] + T + 1)
                        G(v.reciprocal(stat[:, 5:6], stat[:, 4:5]))
                        c.inc(v.tensor_scalar(stat[:, 6:7], stat[:, 2:3], stat[:, 5:6],
                                              consts[:, C_NEG1:C_NEG1 + 1], ALU.mult, ALU.mult), "v_nm", 1)
                        v.wait_ge(sm["s_xn"], b6["s_xn"] + T + 1)
                        if T >= 1:
                            v.wait_ge(sm["t_tr"], b6["t_tr"] + 16 * T)
                        v.tensor_mul(lnz_f, lnsc, gam_b)
                        c.inc(v.tensor_add(lnfin, lnz_f, bet_b), "v_g", 1)
                        for k in range(KC):
                            v.wait_ge(sm["t_tr"], b6["t_tr"] + 16 * T + k + 1)
                            dst = lnz_fm[:, k * 512 + jj * 128: k * 512 + (jj + 1) * 128]
                            c.inc(v.tensor_copy(dst, trb[k % 2][:, :]), "v_xn", 1)
                    for o in range(E):
                        woi = tg * E + o
                        if o == 0 and tg > 0:
                            v.wait_ge(sm["d_out"], 16 * NT + 64 * tg)
                        v.wait_ge(sm["t_wo"], woi + 1)
                        v.wait_ge(sm[f"d_rs{woi % 2}"], 16 * (woi // 2 + 1))
                        c.inc(v.tensor_add(asm1, pab[woi % 2][:, :], rsl[woi % 2]), "v_as", 1)
                        for jj in range(4):
                            v.wait_ge(sm["t_ft"], 4 * woi + jj + 1)
                            c.inc(v.tensor_copy(out_tm[jj][:, o * 128:(o + 1) * 128],
                                                p_zf[jj % 2]), "v_ot", 1)
                    for jj in range(4):
                        c.inc(v.engine_nop(), "v_ox", 1)

            @block.tensor
            def _(t):
                t.wait_ge(sm["s_ev"], ct["s_ev"])
                t.wait_ge(sm["v_sh"], ct["v_sh"])
                for tg in range(NTG):
                    for jj in range(4):
                        T = tg * 4 + jj
                        t.wait_ge(sm["v_g"], b6["v_g"] + 7 * T + 7)
                        for k in range(KC):
                            idx = 16 * T + k
                            if idx >= 1:
                                t.wait_ge(sm["v_xn"], b6["v_xn"] + idx - 1)
                            c.inc(t.transpose(trb[idx % 2][:, :],
                                              lnfin[:, k * 128:(k + 1) * 128], ident_f[:]), "t_tr", 1)
                    for o in range(E):
                        woi = tg * E + o
                        t.wait_ge(sm[f"d_wo{woi % 2}"], 16 * (woi // 2 + 1))
                        t.wait_ge(sm["v_as"], max(0, woi - 1))
                        if o == 0:
                            t.wait_ge(sm["v_xn"], b6["v_xn"] + 64 * (tg + 1))
                        for k in range(KC):
                            ins = t.matmul(pab[woi % 2][:, :], wo_sb[woi % 2][:, k * 128:(k + 1) * 128],
                                           lnz_fm[:, k * 512:(k + 1) * 512],
                                           start=(k == 0), stop=(k == KC - 1))
                        c.inc(ins, "t_wo", 1)
                        t.wait_ge(sm["v_as"], woi + 1)
                        for jj in range(4):
                            t.wait_ge(sm["v_ot"], max(0, 4 * woi + jj - 1))
                            c.inc(t.transpose(p_zf[jj % 2], asm1[:, jj * 128:(jj + 1) * 128],
                                              ident_f[:]), "t_ft", 1)

        if max_phase < 6:
            @block.sync
            def _(sy):
                sy.wait_ge(sm["v_xn"], ct["v_xn"])
                for T in range(NT):
                    c.dma(sy, "d_out", out_t.ap()[T * 128:(T + 1) * 128, :],
                          xn[:, 0:4096].bitcast(F32))

        if debug:
            @block.gpsimd
            def _(g):
                g.wait_ge(sm["v_xn"], ct["v_xn"])
                c.dma(g, "d_dbg", dbg["xn"].ap(), xn[:])
                g.wait_ge(sm["v_cf"], ct["v_cf"])
                c.dma(g, "d_dbg", dbg["comb"].ap(), comb_fm[:])
                if max_phase >= 1:
                    for nm in ("d_kr0", "d_kr1", "d_kt0", "d_kt1", "d_vt0", "d_vt1"):
                        g.wait_ge(sm[nm], ct[nm])
                    c.dma(g, "d_dbg", dbg["kfm"].ap(), kfm_l.ap())
                    c.dma(g, "d_dbg", dbg["ktm"].ap(), ktm_l.ap())
                    c.dma(g, "d_dbg", dbg["vtm"].ap(), vtm_l.ap())
                if max_phase >= 3:
                    for s in range(2):
                        for j in range(2):
                            g.wait_ge(sm[f"zw{s}{j}"], ct[f"zw{s}{j}"])
                    c.dma(g, "d_dbg", dbg["zl"].ap(), z_l.ap())
                if max_phase >= 5:
                    g.wait_ge(sm["d_ac0"], ct["d_ac0"])
                    g.wait_ge(sm["d_ac1"], ct["d_ac1"])
                    c.dma(g, "d_dbg", dbg["routed"].ap(), routed_d.ap())

        @block.sync
        def _(sy):
            sy.wait_ge(sm["d_out"], ct["d_out"])
            sy.wait_ge(sm["d_dbg"], ct["d_dbg"])

    return nc


def make_idx_table(core):
    b, q = core // 4, core % 4
    t = np.zeros((128, N_IDX), np.int32)
    for r in range(4):
        for hl in range(HL):
            h = 4 * q + hl
            t[:, IXC_KFM + r * 4 + hl] = ((4 * b + r) * H + h) * 128 + np.arange(128)
            t[0:16, IXC_KTM + r * 4 + hl] = ((4 * b + r) * H + h) * 16 + np.arange(16)
    for T in range(NT):
        for rh in range(4):
            for hl in range(HL):
                col = IXC_Z + T * 16 + rh * 4 + hl
                cc = np.arange(128) // 16
                pp = np.arange(128) % 16
                cglob = q * CPR + T * 8 + cc
                t[:, col] = (((4 * b + rh) * HL + hl) * 16 + pp) * NCHUNK + cglob
    return t


def host_inputs(inputs):
    x = np.asarray(inputs["L0_attn_x"], np.float32)
    Wk = np.asarray(inputs["L0_attn_Wk"], np.float32)
    Wv = np.asarray(inputs["L0_attn_Wv"], np.float32)
    Wo = np.asarray(inputs["L0_attn_Wo"], np.float32)
    gam = np.asarray(inputs["L0_attn_ln_gamma"], np.float32)
    bet = np.asarray(inputs["L0_attn_ln_beta"], np.float32)
    rw = np.asarray(inputs["L0_ffn_router_w"], np.float32)

    shared = {
        "wk_t": np.ascontiguousarray(Wk.T), "wv_t": np.ascontiguousarray(Wv.T),
        "wo_t": np.ascontiguousarray(Wo.T), "rw_t": np.ascontiguousarray(rw.T),
        "wg_in": np.ascontiguousarray(np.asarray(inputs["L0_ffn_w_gate"], np.float32)),
        "wu_in": np.ascontiguousarray(np.asarray(inputs["L0_ffn_w_up"], np.float32)),
        "wd_in": np.ascontiguousarray(np.asarray(inputs["L0_ffn_w_down"], np.float32)),
        "shg_in": np.ascontiguousarray(np.asarray(inputs["L0_ffn_sh_gate"], np.float32)),
        "shu_in": np.ascontiguousarray(np.asarray(inputs["L0_ffn_sh_up"], np.float32)),
        "shd_in": np.ascontiguousarray(np.asarray(inputs["L0_ffn_sh_down"], np.float32)),
        "sgw_in": np.ascontiguousarray(
            np.asarray(inputs["L0_ffn_sh_gate_w"], np.float32).reshape(KC, 128).T),
        "gam_in": np.ascontiguousarray(np.broadcast_to(gam[None, :], (128, D))),
        "bet_in": np.ascontiguousarray(np.broadcast_to(bet[None, :], (128, D))),
        "ident_in": np.eye(128, dtype=np.float32),
        "ones_in": np.ones((1, 128), np.float32),
    }
    P = np.zeros((128, 128), np.float32)
    P[np.arange(64), np.arange(64) + 64] = -1.0
    P[np.arange(64) + 64, np.arange(64)] = 1.0
    shared["ropeP_in"] = np.ascontiguousarray(P.T)
    cvals = np.zeros(8, np.float32)
    cvals[C_INVD] = 1.0 / D
    cvals[C_EPS] = 1e-5
    cvals[C_NEG1] = -1.0
    cvals[C_TINY] = 1e-9
    cvals[C_NBIG] = -1e9
    cvals[C_ONE] = 1.0
    cvals[C_NINV] = -INV_N
    cvals[C_INVN] = INV_N
    shared["consts_in"] = np.ascontiguousarray(np.broadcast_to(cvals[None, :], (128, 8)))

    inv_freq = 1.0 / (10000.0 ** (np.arange(0, DH, 2, dtype=np.float64) / DH))
    in_maps = []
    for core in range(8):
        b, q = core // 4, core % 4
        pos = q * TOK + np.arange(TOK, dtype=np.float64)
        fr = np.outer(pos, inv_freq)
        emb = np.concatenate([fr, fr], axis=1)
        m = dict(shared)
        m["x_in"] = np.ascontiguousarray(x[b, q * TOK:(q + 1) * TOK])
        m["cos_in"] = np.ascontiguousarray(np.cos(emb).T.astype(np.float32))
        m["sin_in"] = np.ascontiguousarray(np.sin(emb).T.astype(np.float32))
        in_maps.append(m)
    return in_maps


def kernel(**inputs):
    in_maps = host_inputs(inputs)
    nc = build_nc(max_phase=6, debug=False)
    res = run_bass_kernel_spmd(nc, in_maps, core_ids=list(range(8)))
    out = np.empty((B, S, D), np.float32)
    for core in range(8):
        b, q = core // 4, core % 4
        out[b, q * TOK:(q + 1) * TOK] = res.results[core]["out"]
    return out
